# version 05
# baseline (speedup 1.0000x reference)
import numpy as np
import ml_dtypes
from contextlib import ExitStack

import concourse.bass as bass
import concourse.mybir as mybir
from concourse.bass_utils import run_bass_kernel_spmd

F32 = mybir.dt.float32
BF16 = mybir.dt.bfloat16
AF = mybir.ActivationFunctionType
ALU = mybir.AluOpType
BF = ml_dtypes.bfloat16

B, L, D = 2, 4096, 2048
KCONV = 4
TOK = 1024
NHALF = 512
NCHUNK = 8
CH = 128
EPS_M = 1e-6
EPS_LN = 1e-5
NBLK = 16
NPAIR = 16
KT = 16
HBW = NHALF + 4
XPITCH = TOK + 3

ENGINES = ("sync", "gpsimd", "tensor", "vector", "scalar")


TRACE = None
STOP = None


class Sem:

    def __init__(self, handle, name="?"):
        self.h = handle
        self.n = 0
        self.real_n = 0
        self.marks = {}
        self.name = name

    def inc(self, k=1):
        self.n += k
        return self.n

    def token(self):
        return (self.real_n, dict(self.marks))


def build_nc():
    nc = bass.Bass("TRN2", target_bir_lowering=False, num_devices=8)
    ctx = ExitStack()
    EC = ctx.enter_context

    def din(name, shape, dt=F32):
        return nc.dram_tensor(name, shape, dt, kind="ExternalInput")

    xT_d = din("xT", [D, XPITCH])
    Win_d = din("Win", [D, 4224])
    Wout_d = din("Wout", [D, D])
    WffT_d = din("WffT", [D, D])
    b_in_r_d = din("b_in_r", [128, 33])
    cwr_d = din("cwr", [128, 17 * KCONV])
    cbr_d = din("cbr", [128, 17])
    dtb32_d = din("dtb32", [32, 1])
    alog32_d = din("alog32", [32, 1])
    Dcol_d = din("Dcol", [128, NPAIR])
    nw_d = din("nwcol", [128, NBLK])
    fs_d = din("fscol", [128, NBLK])
    fb_d = din("fbcol", [128, NBLK])
    hflag_d = din("hflag", [128, 1])
    ind8_d = din("ind8", [128, 8])
    selm8_d = din("selm8", [8, 8])
    pmask8_d = din("pmask8", [8, 1])
    identf_d = din("identf_d", [128, 128])
    tribias_d = din("tribias_d", [128, 128])
    epsb_d = din("epsb_d", [1, 2])

    out_d = nc.dram_tensor("out", [D, TOK], F32, kind="ExternalOutput")

    ccb = nc.dram_tensor("ccb", [8 * 132, 512], F32)
    shid = nc.dram_tensor("shid", [32, TOK], BF16)
    slod = nc.dram_tensor("slod", [32, TOK], BF16)
    edram = nc.dram_tensor("edram", [32, TOK], BF16)
    cmdram = nc.dram_tensor("cmdram", [32, TOK], BF16)
    e2dram = nc.dram_tensor("e2dram", [32, 8], F32)
    lamdram = nc.dram_tensor("lamdram", [32, 8], F32)
    wdram = nc.dram_tensor("wdram", [32, 8], F32)

    S = {}
    for s in ("CL", "WL", "EV", "CV", "SC", "TR", "XW", "ST", "YP", "MC",
              "EX", "CC", "CCD", "RB", "GA", "GL", "SB", "SBG", "DB", "DB2", "Y2", "RM", "WO", "XR",
              "L1", "FF", "OD", "BAR", "HC", "GB"):
        S[s] = Sem(EC(nc.semaphore("s" + s)), s)
    wload = [Sem(EC(nc.semaphore("wl0")), "wl0"), Sem(EC(nc.semaphore("wl1")), "wl1")]
    wfree = [Sem(EC(nc.semaphore("wf0")), "wf0"), Sem(EC(nc.semaphore("wf1")), "wf1")]

    sb = {}

    def alloc(name, shape, dt=F32):
        sb[name] = EC(nc.sbuf_tensor(name, shape, dt))
        return sb[name]

    for nm, shp, dt in [
        ("cbir", [128, 33], F32), ("cwrb", [128, 68], F32), ("cbrb", [128, 17], F32),
        ("dtbb", [32, 1], F32), ("alogb", [32, 1], F32), ("negA3", [96, 1], F32),
        ("Dcolb", [128, NPAIR], F32), ("nwb", [128, NBLK], F32),
        ("fsb", [128, NBLK], F32), ("fbb", [128, NBLK], F32),
        ("hflagb", [128, 1], F32), ("ind8b", [128, 8], F32),
        ("selm8b", [8, 8], F32), ("pmask8b", [8, 1], F32),
        ("identb", [128, 128], BF16), ("identf", [128, 128], F32),
        ("ones2b", [32, 128], BF16), ("onescol", [128, 1], BF16),
        ("tribias", [128, 128], F32), ("epsb", [1, 2], F32),
        ("xtb", [128, KT * XPITCH], BF16),
        ("hb", [128, 17 * HBW], F32),
        ("fa", [128, 6976], F32),
        ("ba", [128, 7168], BF16),
        ("sttb", [128, 4096], BF16),
        ("sg", [128, NBLK * TOK], BF16),
        ("Y2", [128, NBLK * TOK], BF16),
        ("xsc", [128, 2048], BF16),
        ("bcxc", [64, 132], BF16),
        ("cm0c", [32, 128], BF16),
        ("bctc", [128, 64], BF16),
        ("wsTc", [128, 96], F32),
        ("smlc", [128, 32], F32),
        ("s_hic", [32, 128], BF16), ("s_loc", [32, 128], BF16),
        ("h0s", [64, 64], BF16),
        ("wb0", [128, KT * 128], BF16), ("wb1", [128, KT * 128], BF16),
        ("lamlog", [32, 8], F32), ("lamcum", [32, 8], F32), ("E2c", [32, 8], F32),
        ("lamc", [32, 8], F32), ("lamT", [1, 32], F32), ("lmw", [1, 32], F32),
        ("glamT", [8, 32], F32), ("wcomb", [8, 32], F32),
    ]:
        alloc(nm, shp, dt)

    FA = sb["fa"]
    FAP = 6976
    hrun_ = lambda: FA[:, 0:512]
    scc_ = lambda: FA[:, 512:1024]
    mskw_ = lambda: FA[:, 1024:1536]
    gscr_ = lambda i: FA[:, 1536 + 512 * i: 2048 + 512 * i]
    y1t_ = lambda: FA[:, 2560:2624]
    arms_ = lambda: FA[:, 0:512]
    Ab_ = lambda: FA[:, 512:1024]
    Bb_ = lambda: FA[:, 1024:1536]
    xr_ = lambda: FA[:, 1536:2048]
    ost_ = lambda: FA[:, 2048:2560]
    E2rep_ = lambda: FA[:, 2624:2752]
    lamrep_ = lambda: FA[:, 2752:2880]
    wrep_ = lambda: FA[:, 2880:3008]
    stat0_ = lambda: FA[0:1, 3072:3584]
    stat1_ = lambda: FA[0:1, 3584:4096]
    stat2_ = lambda: FA[0:1, 4096:4608]
    onesrow_ = lambda: FA[0:1, 4608:4736]
    y1b_ = lambda: FA[:, 4736:4864]
    wsstk_ = lambda: FA[0:96, 4864:4992]
    dargb_ = lambda: FA[:, 4992:5504]
    swA_ = lambda: FA[32:64, 5504:5696]
    swB_ = lambda: FA[0:32, 5504:5696]
    dtraw_ = lambda: FA[0:32, 5696:5824]
    srep_ = lambda: FA[32:64, 5696:5824]
    dtb_ = lambda: FA[64:96, 5696:5824]
    eS_ = lambda: FA[0:32, 5824:5952]
    lndt_ = lambda: FA[32:64, 5824:5952]
    wtil_ = lambda: FA[64:96, 5824:5952]
    conva_ = lambda: FA[:, 5952:6464]
    convb_ = lambda: FA[:, 6464:6976]

    BA = sb["ba"]
    BAP = 7168
    sfhl_ = lambda: BA[0:2, 0:4096]
    sfhl32_ = lambda: BA[0:32, 0:4096]
    XTc_ = lambda: BA[:, 4096:6144]
    gbuf_ = lambda: BA[:, 6144:6272]
    dexpb_ = lambda: BA[:, 6272:6784]
    mbuf_ = lambda i: BA[:, 6784 + 128 * i: 6912 + 128 * i]
    xtw_ = lambda: BA[:, 7040:7168]
    EB2_ = lambda: BA[0:64, 0:2048]
    CMR_ = lambda: BA[0:64, 2048:4096]
    chat_ = lambda: BA[0:64, 4096:6144]
    rb16_ = lambda: BA[:, 0:512]
    rsq16_ = lambda: BA[:, 512:1024]

    r16_ = lambda: sb["xtb"][:, 0:8192]
    rH_ = lambda: sb["hb"][:, 0:8192]
    STB = sb["sttb"]

    ops = {e: [] for e in ENGINES}
    stopped = [False]

    def checkpoint(label):
        if STOP == label:
            stopped[0] = True

    SSELF = {e: Sem(EC(nc.semaphore("self_" + e)), "self_" + e)
             for e in ("vector", "scalar", "gpsimd")}

    tpace = [0]

    def em(eng, fn, inc=None, k=1, inc2=None, k2=1, dma=False):
        if stopped[0]:
            return
        if eng == "tensor":
            tpace[0] += 1
            if tpace[0] % 4 == 0:
                for _ in range(16):
                    ops["tensor"].append(lambda E: E.nop())
        selfs = SSELF.get(eng) if not dma else None
        if selfs is not None:
            if selfs.n > 0:
                vv = selfs.n
                if TRACE is not None:
                    TRACE[eng].append(("wait", selfs.name, vv))
                ops[eng].append(lambda E, s=selfs, v=vv: E.wait_ge(s.h, v))
            for s_, k_ in ((inc, k), (inc2, k2)):
                if s_ is not None:
                    s_.inc(k_)
                    s_.marks[eng] = selfs.n + 1
            selfs.inc(1)
            if TRACE is not None:
                TRACE[eng].append(("inc", selfs.name, 1))

            def wrapped(E, fn=fn, s=selfs):
                fn(E).then_inc(s.h, 1)
            ops[eng].append(wrapped)
            return
        assert inc2 is None or inc is None, "two real incs on one instruction"
        real = inc if inc is not None else inc2
        rk = k if inc is not None else k2
        if real is not None:
            real.inc(rk)
            real.real_n += rk
            if TRACE is not None:
                TRACE[eng].append(("inc", real.name, rk))

            def wrapped(E, fn=fn, s=real, kk=rk):
                fn(E).then_inc(s.h, kk)
            ops[eng].append(wrapped)
        else:
            ops[eng].append(fn)

    def _emit_wait(eng, sem, v):
        if v > 0:
            if TRACE is not None:
                TRACE[eng].append(("wait", sem.name, v))
            ops[eng].append(lambda E, s=sem, vv=v: E.wait_ge(s.h, vv))

    def wt(eng, sem, val=None):
        if stopped[0]:
            return
        if isinstance(val, tuple):
            rn, marks = val
        elif val is None:
            rn, marks = sem.real_n, sem.marks
        else:
            rn, marks = val, {}
        _emit_wait(eng, sem, rn)
        for e, m in marks.items():
            if eng == e:
                continue
            _emit_wait(eng, SSELF[e], m)

    def barrier():
        if stopped[0]:
            return
        S["BAR"].inc(5)
        tgt = S["BAR"].n
        for e in ENGINES:
            if TRACE is not None:
                TRACE[e].append(("inc", "BAR", 1))
                TRACE[e].append(("wait", "BAR", tgt))
            ops[e].append(lambda E: E.sem_inc(S["BAR"].h, 1))
            ops[e].append(lambda E, t=tgt: E.wait_ge(S["BAR"].h, t))

    for dst, src in [("cbir", b_in_r_d), ("cwrb", cwr_d), ("cbrb", cbr_d),
                     ("dtbb", dtb32_d), ("alogb", alog32_d), ("Dcolb", Dcol_d),
                     ("nwb", nw_d), ("fsb", fs_d), ("fbb", fb_d),
                     ("hflagb", hflag_d), ("ind8b", ind8_d), ("selm8b", selm8_d),
                     ("pmask8b", pmask8_d),
                     ("identf", identf_d),
                     ("tribias", tribias_d), ("epsb", epsb_d)]:
        em("sync", lambda E, d=sb[dst], s=src: E.dma_start(out=d[:, :], in_=s[:, :]),
           inc=S["CL"], k=16)

    def ld_xtb(E):
        src = bass.AP(xT_d, 0, [[XPITCH, 128], [128 * XPITCH, KT], [1, XPITCH]])
        dst = bass.AP(sb["xtb"], 0, [[KT * XPITCH, 128], [XPITCH, KT], [1, XPITCH]])
        return E.dma_start(out=dst, in_=src)
    em("gpsimd", lambda E: E.memset(sb["ba"][0:32, 0:4096], 0.0))
    em("gpsimd", ld_xtb, inc=S["WL"], k=16, dma=True)

    wt("scalar", S["CL"])
    em("scalar", lambda E: E.activation(sb["negA3"][0:32, :], sb["alogb"][:, :],
                                        AF.Exp), inc=S["SC"])
    wt("vector", S["CL"])
    em("vector", lambda E: E.tensor_copy(sb["identb"][:, :], sb["identf"][:, :]))
    em("vector", lambda E: E.memset(sb["ones2b"][:, :], 1.0))
    em("vector", lambda E: E.memset(sb["onescol"][:, :], 1.0), inc=S["CV"])
    wt("vector", S["SC"])
    em("vector", lambda E: E.tensor_scalar_mul(sb["negA3"][0:32, :],
                                               sb["negA3"][0:32, :], -1.0))
    em("vector", lambda E: E.tensor_copy(sb["negA3"][64:96, :], sb["negA3"][0:32, :]),
       inc=S["CV"])
    wt("vector", S["CL"])
    em("vector", lambda E: E.memset(FA[0:64, 5504:5696], 0.0), inc=S["CV"])

    checkpoint("s0")
    wl_count = [0]
    wfree_tok = [None, None]

    def stream_weight(wdram_t, otile):
        i = wl_count[0] % 2
        buf = sb["wb0"] if i == 0 else sb["wb1"]
        if wl_count[0] >= 2 and wfree_tok[i] is not None:
            wt("gpsimd", wfree_tok[i][0], wfree_tok[i][1])

        def ld(E, w=wdram_t, o=otile, b=buf):
            src = bass.AP(w, o * 128, [[w.shape[1], 128], [128 * w.shape[1], KT], [1, 128]])
            dst = bass.AP(b, 0, [[KT * 128, 128], [128, KT], [1, 128]])
            return E.dma_start(out=dst, in_=src)
        em("gpsimd", ld, inc=wload[i], k=16, dma=True)
        wl_count[0] += 1
        return buf, wload[i], wload[i].n

    def mark_wfree(sem):
        i = (wl_count[0] - 1) % 2
        wfree_tok[i] = (sem, sem.token())

    rP = ExitStack()
    pp = [rP.enter_context(nc.psum_tensor("pp0", [128, 512])),
          rP.enter_context(nc.psum_tensor("pp1", [128, 512]))]
    ptr = rP.enter_context(nc.psum_tensor("ptr", [128, 128], BF16))
    ptrf = rP.enter_context(nc.psum_tensor("ptrf", [128, 128]))
    psb = [rP.enter_context(nc.psum_tensor("psb0", [128, 512])),
           rP.enter_context(nc.psum_tensor("psb1", [128, 512]))]
    py = [rP.enter_context(nc.psum_tensor("py0", [128, 128])),
          rP.enter_context(nc.psum_tensor("py1", [128, 128]))]

    pp_guard = [0, 0]
    halo_guard = [0]
    ptr_guard = [0]
    ptrf_guard = [0]
    psb_guard = [0, 0]
    py_guard = [None, None]

    wt("tensor", S["CL"])
    wt("tensor", S["WL"], 16)

    def proj_half(half):
        for j in range(17):
            ot = 16 + j
            buf, lsem, ltgt = stream_weight(Win_d, ot)
            p = pp[j % 2]
            wt("tensor", lsem, ltgt)
            if pp_guard[j % 2]:
                wt("tensor", S["EV"], pp_guard[j % 2])
            for k in range(KT):
                last = (k == KT - 1)
                em("tensor", lambda E, b=buf, kk=k, p_=p, h=half:
                   E.matmul(p_[:, 0:512], b[:, 128 * kk:128 * (kk + 1)],
                            sb["xtb"][:, kk * XPITCH + 3 + NHALF * h:
                                      kk * XPITCH + 3 + NHALF * h + 512],
                            start=(kk == 0), stop=(kk == KT - 1)),
                   inc=S["YP"] if last else None)
            if half == 0:
                if halo_guard[0]:
                    wt("tensor", S["EV"], halo_guard[0])
                for k in range(KT):
                    last = (k == KT - 1)
                    em("tensor", lambda E, b=buf, kk=k:
                       E.matmul(ptrf[:, 96:99], b[:, 128 * kk:128 * (kk + 1)],
                                sb["xtb"][:, kk * XPITCH: kk * XPITCH + 3],
                                start=(kk == 0), stop=(kk == KT - 1)),
                       inc=S["YP"] if last else None)
            mark_wfree(S["YP"])
            wt("vector", S["YP"])
            em("vector", lambda E, jj=j, p_=p, ot_=ot:
               E.tensor_scalar(sb["hb"][:, jj * HBW + 4: jj * HBW + 516], p_[:, 0:512],
                               sb["cbir"][:, ot_:ot_ + 1], None, ALU.add), inc=S["EV"])
            pp_guard[j % 2] = S["EV"].token()
            if half == 0:
                em("vector", lambda E, jj=j, ot_=ot:
                   E.tensor_scalar(sb["hb"][:, jj * HBW + 1: jj * HBW + 4],
                                   ptrf[:, 96:99],
                                   sb["cbir"][:, ot_:ot_ + 1], sb["hflagb"][:, 0:1],
                                   ALU.add, ALU.mult), inc=S["EV"])
                halo_guard[0] = S["EV"].token()
            checkpoint("pj%d_%d" % (half, j))

    def chunk_pipeline(c):
        cl = c % 4
        for j in range(17):
            rows = 128 if j < 16 else 64
            base = j * HBW
            acc = conva_() if j % 2 == 0 else convb_()
            wt("vector", S["EV"])
            if j >= 2:
                wt("vector", S["SC"])
            em("vector", lambda E, jj=j, r=rows, b=base, a=acc, w0=1 + cl * CH:
               E.tensor_scalar(a[0:r, 0:CH], sb["hb"][0:r, b + w0: b + w0 + CH],
                               sb["cwrb"][0:r, 4 * jj: 4 * jj + 1],
                               sb["cbrb"][0:r, jj: jj + 1], ALU.mult, ALU.add))
            for k in range(1, 4):
                em("vector", lambda E, jj=j, r=rows, b=base, kk=k, a=acc, w0=1 + cl * CH:
                   E.scalar_tensor_tensor(a[0:r, 0:CH],
                                          sb["hb"][0:r, b + w0 + kk: b + w0 + kk + CH],
                                          sb["cwrb"][0:r, 4 * jj + kk: 4 * jj + kk + 1],
                                          a[0:r, 0:CH], ALU.mult, ALU.add),
                   inc=S["CV"] if k == 3 else None)
            wt("scalar", S["CV"])
            if j < 16:
                wt("scalar", S["Y2"])
                em("scalar", lambda E, jj=j, a=acc:
                   E.activation(sb["xsc"][:, jj * CH:(jj + 1) * CH], a[:, 0:CH],
                                AF.Silu), inc=S["SC"])
            else:
                em("scalar", lambda E, a=acc:
                   E.activation(sb["bcxc"][:, 0:CH], a[0:64, 0:CH], AF.Silu),
                   inc=S["SC"])
        checkpoint("cv%d" % c)
        wt("vector", S["SC"])
        wt("vector", S["SB"])
        wt("vector", S["SBG"])
        em("vector", lambda E, w0=4 + cl * CH:
           E.tensor_copy(dtraw_(), sb["hb"][64:96, 16 * HBW + w0: 16 * HBW + w0 + CH]),
           inc=S["CV"])
        wt("scalar", S["CV"])
        em("scalar", lambda E: E.activation(dtb_(), dtraw_(), AF.Exp,
                                            bias=sb["dtbb"][:, 0:1]))
        em("scalar", lambda E: E.activation(dtb_(), dtb_(), AF.Ln, bias=1.0))
        em("scalar", lambda E: E.activation(lndt_(), dtb_(), AF.Ln), inc=S["SC"])
        wt("vector", S["SC"])
        em("vector", lambda E: E.tensor_scalar(FA[32:64, 5504 + 64:5504 + 192],
                                               dtb_(), sb["negA3"][64:96, 0:1],
                                               None, ALU.mult))
        parity = 0
        for k in [1, 2, 4, 8, 16, 32, 64]:
            srcb = swA_() if parity == 0 else swB_()
            dstb = swB_() if parity == 0 else swA_()
            srow = 32 if parity == 0 else 0
            em("vector", lambda E, sr=srow, kk=k:
               E.tensor_tensor(FA[32 - sr:64 - sr, 5504 + 64:5504 + 192],
                               FA[sr:sr + 32, 5504 + 64:5504 + 192],
                               FA[sr:sr + 32, 5504 + 64 - kk:5504 + 192 - kk],
                               ALU.add))
            parity ^= 1
        em("vector", lambda E: E.tensor_copy(sb["s_hic"][:, :],
                                             FA[0:32, 5504 + 64:5504 + 192]))
        em("vector", lambda E: E.tensor_tensor(sb["s_loc"][:, :],
                                               FA[0:32, 5504 + 64:5504 + 192],
                                               sb["s_hic"][:, :], ALU.subtract))
        em("vector", lambda E: E.tensor_tensor(srep_(), sb["s_hic"][:, :],
                                               sb["s_loc"][:, :], ALU.add))
        em("vector", lambda E: E.tensor_scalar(FA[64:96, 5824:5952], srep_(),
                                               srep_()[:, CH - 1: CH], None,
                                               ALU.subtract))
        em("vector", lambda E, cc=c: E.tensor_copy(sb["lamlog"][:, cc:cc + 1],
                                                   srep_()[:, CH - 1: CH]))
        em("vector", lambda E: E.tensor_copy(sb["cm0c"][:, :],
                                             sb["bcxc"][32:64, 0:CH]), inc=S["CV"])
        wt("scalar", S["CV"])
        em("scalar", lambda E: E.activation(eS_(), srep_(), AF.Exp))
        em("scalar", lambda E: E.activation(wtil_(), wtil_(), AF.Exp, scale=-1.0),
           inc=S["SC"])
        wt("vector", S["SC"])
        em("vector", lambda E: E.tensor_tensor(wtil_(), wtil_(), dtb_(), ALU.mult),
           inc=S["CV"])
        wt("sync", S["CV"])
        em("sync", lambda E, cc=c: E.dma_start(out=shid[:, cc * CH:(cc + 1) * CH],
                                               in_=sb["s_hic"][:, :]), inc=S["SB"], k=16)
        em("sync", lambda E, cc=c: E.dma_start(out=slod[:, cc * CH:(cc + 1) * CH],
                                               in_=sb["s_loc"][:, :]), inc=S["SB"], k=16)
        wt("gpsimd", S["SC"])
        em("gpsimd", lambda E, cc=c: E.dma_start(out=edram[:, cc * CH:(cc + 1) * CH],
                                                 in_=eS_()), inc=S["SBG"], k=16, dma=True)
        em("sync", lambda E, cc=c: E.dma_start(out=cmdram[:, cc * CH:(cc + 1) * CH],
                                               in_=sb["cm0c"][:, :]), inc=S["SB"], k=16)
        checkpoint("dt%d" % c)
        wt("vector", S["CV"])
        if c > 0:
            wt("vector", S["MC"])
        em("vector", lambda E: E.tensor_copy(wsstk_()[0:32, :], srep_()))
        em("vector", lambda E: E.tensor_copy(wsstk_()[32:64, :], wtil_()))
        em("vector", lambda E: E.tensor_copy(wsstk_()[64:96, :], lndt_()),
           inc=S["XW"])
        wt("tensor", S["XW"])
        if ptrf_guard[0]:
            wt("tensor", S["MC"], ptrf_guard[0])
        wt("tensor", S["EV"])
        em("tensor", lambda E: E.transpose(ptrf[:, 0:96], wsstk_()[:, :],
                                           sb["identf"][0:96, 0:96]), inc=S["YP"])
        wt("vector", S["YP"])
        em("vector", lambda E: E.tensor_copy(sb["wsTc"][:, :], ptrf[:, 0:96]),
           inc=S["MC"])
        ptrf_guard[0] = S["MC"].token()
        em("vector", lambda E: E.tensor_tensor(sb["smlc"][:, :], sb["wsTc"][:, 0:32],
                                               sb["wsTc"][:, 64:96], ALU.subtract),
           inc=S["MC"])
        wt("tensor", S["MC"])
        if ptr_guard[0]:
            wt("tensor", S["TR"], ptr_guard[0])
        em("tensor", lambda E: E.transpose(ptr[:, 0:64], sb["bcxc"][:, 0:CH],
                                           sb["identb"][0:64, 0:64]), inc=S["YP"])
        wt("vector", S["YP"])
        em("vector", lambda E: E.tensor_copy(sb["bctc"][:, :], ptr[:, 0:64]),
           inc=S["TR"])
        ptr_guard[0] = S["TR"].token()
        for j in range(NBLK):
            wt("tensor", S["TR"], ptr_guard[0])
            em("tensor", lambda E, jj=j:
               E.transpose(ptr[:, :], sb["xsc"][:, jj * CH:(jj + 1) * CH],
                           sb["identb"][:, :]), inc=S["YP"])
            wt("vector", S["YP"])
            em("vector", lambda E, jj=j:
               E.tensor_copy(XTc_()[:, jj * 128:(jj + 1) * 128], ptr[:, :]),
               inc=S["TR"])
            ptr_guard[0] = S["TR"].token()
        checkpoint("tp%d" % c)
        for p in range(NPAIR):
            wt("vector", S["TR"])
            wt("vector", S["YP"])
            for m in range(2):
                hh = 2 * p + m
                em("vector", lambda E, pp_=p, mm=m, h_=hh:
                   E.tensor_scalar(xtw_()[:, 64 * mm: 64 * (mm + 1)],
                                   XTc_()[:, 128 * pp_ + 64 * mm:
                                          128 * pp_ + 64 * (mm + 1)],
                                   sb["wsTc"][:, 32 + h_: 33 + h_], None, ALU.mult),
                   inc=S["XW"] if m == 1 else None)
            pz = py[p % 2]
            wt("tensor", S["XW"])
            if py_guard[p % 2]:
                gsem, gval = py_guard[p % 2]
                wt("tensor", gsem, gval)
            for m in range(2):
                em("tensor", lambda E, mm=m, pz_=pz:
                   E.matmul(pz_[32 * mm: 32 * (mm + 1), 0:64], sb["bctc"][:, 0:32],
                            xtw_()[:, 64 * mm: 64 * (mm + 1)],
                            start=True, stop=True),
                   inc=S["YP"])
            rows = slice(0, 64) if p < 8 else slice(64, 128)
            cb = (p % 8) * 8 + c
            wt("vector", S["YP"])
            em("vector", lambda E, pz_=pz, rr=rows, cb_=cb:
               E.tensor_copy(STB[rr, 64 * cb_: 64 * (cb_ + 1)], pz_[0:64, 0:64]),
               inc=S["ST"])
            py_guard[p % 2] = (S["ST"], S["ST"].token())
        checkpoint("st%d" % c)
        wt("tensor", S["CV"])
        if ptrf_guard[0]:
            wt("tensor", S["MC"], ptrf_guard[0])
        em("tensor", lambda E: E.matmul(ptrf[:, :], sb["bcxc"][0:32, 0:CH],
                                        sb["cm0c"][:, :], start=True, stop=True),
           inc=S["YP"])
        wt("vector", S["YP"])
        em("vector", lambda E: E.tensor_copy(gbuf_(), ptrf[:, :]), inc=S["MC"])
        ptrf_guard[0] = S["MC"].token()
        checkpoint("g%d" % c)
        if c == 0:
            wt("gpsimd", SSELF["gpsimd"], 1)
        wt("gpsimd", S["SB"])
        wt("gpsimd", S["SBG"])
        if c > 0:
            wt("gpsimd", S["MC"])
        em("gpsimd", lambda E, cc=c: E.dma_start(
            out=sfhl_()[0:1, :],
            in_=bass.AP(shid, cc * CH, [[1, 1], [TOK, 32], [1, CH]])), inc=S["DB"], k=16, dma=True)
        em("gpsimd", lambda E, cc=c: E.dma_start(
            out=sfhl_()[1:2, :],
            in_=bass.AP(slod, cc * CH, [[1, 1], [TOK, 32], [1, CH]])), inc=S["DB"], k=16, dma=True)
        checkpoint("fl%d" % c)
        for g in range(8):
            pz = psb[g % 2]
            wt("tensor", S["DB"])
            if psb_guard[g % 2]:
                wt("tensor", S["MC"], psb_guard[g % 2])
            em("tensor", lambda E, gg=g, pz_=pz:
               E.matmul(pz_[:, 0:512], sb["ones2b"][:, :],
                        sfhl32_()[:, 512 * gg: 512 * (gg + 1)],
                        start=True, stop=True), inc=S["YP"])
            if g == 0:
                checkpoint("sb%d" % c)
            wt("vector", S["YP"])
            wt("vector", S["MC"])
            for hq in range(4):
                hh = 4 * g + hq
                em("vector", lambda E, q=hq, h_=hh, pz_=pz:
                   E.scalar_tensor_tensor(dargb_()[:, 128 * q: 128 * (q + 1)],
                                          pz_[:, 128 * q: 128 * (q + 1)],
                                          sb["smlc"][:, h_: h_ + 1],
                                          sb["tribias"][:, :], ALU.subtract, ALU.add),
                   inc=S["MC"] if hq == 3 else None)
            psb_guard[g % 2] = S["MC"].token()
            if g == 0:
                checkpoint("da%d" % c)
            wt("scalar", S["MC"])
            em("scalar", lambda E: E.activation(dexpb_(), dargb_(), AF.Exp),
               inc=S["SC"])
            if g == 0:
                checkpoint("dx%d" % c)
            for sp in range(2):
                p_ = 2 * g + sp
                yz = py[p_ % 2]
                wt("vector", S["SC"])
                for m in range(2):
                    em("vector", lambda E, q=2 * sp + m, mm=m:
                       E.tensor_tensor(mbuf_(mm), gbuf_(),
                                       dexpb_()[:, 128 * q: 128 * (q + 1)],
                                       ALU.mult),
                       inc=S["MC"] if m == 1 else None)
                if g == 0 and sp == 0:
                    checkpoint("mb%d" % c)
                wt("tensor", S["MC"])
                if py_guard[p_ % 2]:
                    gsem, gval = py_guard[p_ % 2]
                    wt("tensor", gsem, gval)
                for m in range(2):
                    em("tensor", lambda E, pp_=p_, mm=m, yz_=yz:
                       E.matmul(yz_[64 * mm: 64 * (mm + 1), :],
                                XTc_()[:, 128 * pp_ + 64 * mm:
                                       128 * pp_ + 64 * (mm + 1)],
                                mbuf_(mm), start=True, stop=True),
                       inc=S["YP"])
                if g == 0 and sp == 0:
                    checkpoint("ym%d" % c)
                wt("vector", S["YP"])
                em("vector", lambda E, cc=c, pp_=p_, yz_=yz:
                   E.scalar_tensor_tensor(
                       sb["Y2"][:, pp_ * TOK + cc * CH: pp_ * TOK + (cc + 1) * CH],
                       sb["xsc"][:, 128 * pp_: 128 * (pp_ + 1)],
                       sb["Dcolb"][:, pp_:pp_ + 1], yz_[:, :], ALU.mult, ALU.add),
                   inc=S["Y2"])
                py_guard[p_ % 2] = (S["Y2"], S["Y2"].token())
                if g == 0 and sp == 0:
                    checkpoint("as%d" % c)
            if g == 0:
                checkpoint("ga%d" % c)
            if g == 1:
                checkpoint("gb%d" % c)
            if g == 3:
                checkpoint("gc%d" % c)
            if g == 5:
                checkpoint("gd%d" % c)
        em("scalar", lambda E: E.nop(), inc=S["Y2"])

    proj_half(0)
    checkpoint("p1a")
    wt("vector", S["EV"])
    for c in range(4):
        chunk_pipeline(c)
        checkpoint("c%d" % c)
    wt("vector", S["Y2"])
    em("vector", lambda E:
       E.tensor_copy(bass.AP(sb["hb"], 1, [[17 * HBW, 128], [HBW, 17], [1, 3]]),
                     bass.AP(sb["hb"], 513, [[17 * HBW, 128], [HBW, 17], [1, 3]])),
       inc=S["HC"])
    proj_half(1)
    checkpoint("p1b")
    wt("vector", S["EV"])
    for c in range(4, 8):
        chunk_pipeline(c)
        checkpoint("c%d" % c)

    wt("vector", S["CV"])
    em("vector", lambda E: E.memset(sb["lamcum"][:, :], 0.0))
    for c in range(NCHUNK - 2, -1, -1):
        em("vector", lambda E, cc=c: E.tensor_tensor(
            sb["lamcum"][:, cc:cc + 1], sb["lamcum"][:, cc + 1:cc + 2],
            sb["lamlog"][:, cc + 1:cc + 2], ALU.add))
    em("vector", lambda E: E.nop(), inc=S["EX"])
    wt("scalar", S["EX"])
    em("scalar", lambda E: E.activation(sb["E2c"][:, :], sb["lamcum"][:, :], AF.Exp))
    em("scalar", lambda E: E.activation(sb["lamc"][:, :], sb["lamlog"][:, :], AF.Exp),
       inc=S["SC"])
    wt("vector", S["SC"])
    em("vector", lambda E: E.tensor_tensor(sb["lamcum"][:, 0:1], sb["lamcum"][:, 0:1],
                                           sb["lamlog"][:, 0:1], ALU.add), inc=S["EX"])
    wt("tensor", S["EX"])
    if ptrf_guard[0]:
        wt("tensor", S["MC"], ptrf_guard[0])
    em("tensor", lambda E: E.transpose(ptrf[0:1, 0:32], sb["lamcum"][:, 0:1],
                                       sb["identf"][0:32, 0:32]), inc=S["YP"])
    wt("vector", S["YP"])
    em("vector", lambda E: E.tensor_copy(sb["lamT"][:, :], ptrf[0:1, 0:32]), inc=S["EX"])
    wt("sync", S["SC"])
    em("sync", lambda E: E.dma_start(out=e2dram[:, :], in_=sb["E2c"][:, :]),
       inc=S["RB"], k=16)
    em("sync", lambda E: E.dma_start(out=lamdram[:, :], in_=sb["lamc"][:, :]),
       inc=S["RB"], k=16)
    wt("sync", S["RB"])
    for nm, dram in (("E2", e2dram), ("lam", lamdram)):
        dstmap = {"E2": E2rep_, "lam": lamrep_}[nm]
        for hh in range(2):
            for m in range(2):
                em("sync", lambda E, f_=dstmap, d_=dram, h_=hh, mm=m: E.dma_start(
                    out=bass.AP(sb["fa"], (64 * h_ + 32 * mm) * FAP +
                                (2624 if f_ is E2rep_ else 2752),
                                [[FAP, 32], [8, 16], [1, 8]]),
                    in_=bass.AP(d_, 8 * mm, [[0, 32], [16, 16], [1, 8]])),
                    inc=S["RB"], k=16)
    wt("vector", S["RB"])
    wt("vector", S["ST"])
    for p in range(NPAIR):
        rows = slice(0, 64) if p < 8 else slice(64, 128)
        pc = 64 * (p % 8)
        for c in range(NCHUNK):
            cb = (p % 8) * 8 + c
            if c == 0:
                em("vector", lambda E, rr=rows, pc_=pc, cb_=cb, pp_=p:
                   E.tensor_scalar(scc_()[rr, pc_: pc_ + 64],
                                   STB[rr, 64 * cb_: 64 * (cb_ + 1)],
                                   E2rep_()[rr, pp_ * 8: pp_ * 8 + 1],
                                   None, ALU.mult))
            else:
                em("vector", lambda E, rr=rows, pc_=pc, cb_=cb, pp_=p, cc=c:
                   E.scalar_tensor_tensor(scc_()[rr, pc_: pc_ + 64],
                                          STB[rr, 64 * cb_: 64 * (cb_ + 1)],
                                          E2rep_()[rr, pp_ * 8 + cc: pp_ * 8 + cc + 1],
                                          scc_()[rr, pc_: pc_ + 64],
                                          ALU.mult, ALU.add))
    em("vector", lambda E: E.memset(FA[0:3, 1536:2048], 0.0))
    em("vector", lambda E: E.nop(), inc=S["EX"])
    for q in range(8):
        wt("vector", S["EX"])
        if q > 0:
            wt("vector", S["CC"])
        em("vector", lambda E, qq=q: E.tensor_scalar(mskw_()[:, :], scc_()[:, :],
                                                     sb["ind8b"][:, qq:qq + 1],
                                                     None, ALU.mult))
        em("vector", lambda E, qq=q: E.tensor_scalar(sb["lmw"][:, :], sb["lamT"][:, :],
                                                     sb["ind8b"][0:1, qq:qq + 1],
                                                     None, ALU.mult), inc=S["EX"])
        wt("gpsimd", S["EX"])
        em("gpsimd", lambda E, qq=q: E.dma_start(out=ccb[132 * qq: 132 * qq + 128, :],
                                                 in_=mskw_()[:, :]), inc=S["CC"], k=16, dma=True)
        em("gpsimd", lambda E, qq=q: E.dma_start(
            out=ccb[132 * qq + 128: 132 * qq + 129, 0:32],
            in_=sb["lmw"][:, :]), inc=S["CC"], k=16, dma=True)
        em("gpsimd", lambda E, qq=q: E.dma_start(
            out=ccb[132 * qq + 128: 132 * qq + 129, 32:512],
            in_=FA[0:1, 1536:2016]), inc=S["CC"], k=16, dma=True)
        em("gpsimd", lambda E, qq=q: E.dma_start(
            out=ccb[132 * qq + 129: 132 * qq + 132, :],
            in_=FA[0:3, 1536:2048]), inc=S["CC"], k=16, dma=True)
    wt("gpsimd", S["CC"])
    em("gpsimd", lambda E: E.collective_compute(
        "AllReduce", ALU.add, replica_groups=[list(range(8))],
        ins=[bass.AP(ccb, 0, [[1, 1], [1, 8 * 132 * 512]])],
        outs=[bass.AP(ccb, 0, [[1, 1], [1, 8 * 132 * 512]])],
        cc_dim="Partition"), inc=S["CCD"], k=1, dma=True)
    CC_DONE = S["CCD"].token()
    checkpoint("cc")

    barrier()
    checkpoint("exch")

    pgt_guard = [0, 0]
    for j in range(NBLK):
        buf, lsem, ltgt = stream_weight(Win_d, j)
        wt("tensor", lsem, ltgt)
        if pgt_guard[j % 2]:
            wt("tensor", S["SC"], pgt_guard[j % 2])
        for hh in range(2):
            p_ = pp[j % 2] if hh == 0 else psb[j % 2]
            for k in range(KT):
                last = (k == KT - 1)
                em("tensor", lambda E, b=buf, kk=k, pz=p_, h=hh:
                   E.matmul(pz[:, 0:512], b[:, 128 * kk:128 * (kk + 1)],
                            sb["xtb"][:, kk * XPITCH + 3 + NHALF * h:
                                      kk * XPITCH + 3 + NHALF * h + 512],
                            start=(kk == 0), stop=(kk == KT - 1)),
                   inc=S["GA"] if last else None)
        mark_wfree(S["GA"])
        wt("vector", S["GA"])
        for hh in range(2):
            pz = pp[j % 2] if hh == 0 else psb[j % 2]
            em("vector", lambda E, jj=j, h=hh, pz_=pz:
               E.tensor_scalar(FA[:, 512 * (h + 1): 512 * (h + 2)], pz_[:, 0:512],
                               sb["cbir"][:, jj:jj + 1], None, ALU.add),
               inc=S["EV"])
        pgt_guard[j % 2] = S["EV"].token()
        wt("scalar", S["EV"])
        for hh in range(2):
            em("scalar", lambda E, jj=j, h=hh:
               E.activation(sb["sg"][:, jj * TOK + NHALF * h: jj * TOK + NHALF * h + 512],
                            FA[:, 512 * (h + 1): 512 * (h + 2)], AF.Silu), inc=S["SC"])
        checkpoint("gate%d" % j)

    wt("sync", S["CCD"], CC_DONE)
    em("sync", lambda E: E.dma_start(
        out=sb["glamT"][:, :],
        in_=bass.AP(ccb, 128 * 512, [[132 * 512, 8], [1, 32]])), inc=S["GL"], k=16)
    wt("tensor", S["GL"])
    wt("tensor", S["GA"])
    wt("tensor", S["MC"], ptrf_guard[0])
    em("tensor", lambda E: E.matmul(ptrf[0:8, 0:32], sb["selm8b"][:, :],
                                    sb["glamT"][:, :], start=True, stop=True),
       inc=S["YP"])
    wt("scalar", S["YP"])
    em("scalar", lambda E: E.activation(sb["wcomb"][:, :], ptrf[0:8, 0:32], AF.Exp),
       inc=S["SC"])
    wt("vector", S["SC"])
    em("vector", lambda E: E.tensor_scalar(sb["wcomb"][:, :], sb["wcomb"][:, :],
                                           sb["pmask8b"][:, 0:1], None, ALU.mult),
       inc=S["EX"])
    wt("tensor", S["EX"])
    em("tensor", lambda E: E.transpose(ptrf[0:32, 32:40], sb["wcomb"][:, :],
                                       sb["identf"][0:8, 0:8]), inc=S["YP"])
    wt("vector", S["YP"])
    em("vector", lambda E: E.tensor_copy(sb["lamc"][:, :], ptrf[0:32, 32:40]),
       inc=S["EX"])
    wt("sync", S["EX"])
    em("sync", lambda E: E.dma_start(out=wdram[:, :], in_=sb["lamc"][:, :]),
       inc=S["RB"], k=16)
    wt("sync", S["RB"])
    for hh in range(2):
        for m in range(2):
            em("sync", lambda E, h_=hh, mm=m: E.dma_start(
                out=bass.AP(sb["fa"], (64 * h_ + 32 * mm) * FAP + 2880,
                            [[FAP, 32], [8, 16], [1, 8]]),
                in_=bass.AP(wdram, 8 * mm, [[0, 32], [16, 16], [1, 8]])),
                inc=S["RB"], k=16)
    wt("vector", S["RB"])
    gscr_done = [0, 0]
    for r in range(8):
        gi = r % 2
        wt("sync", S["CCD"], CC_DONE)
        if gscr_done[gi]:
            wt("sync", S["GB"], gscr_done[gi])
        em("sync", lambda E, rr=r, g_=gi: E.dma_start(
            out=gscr_(g_), in_=ccb[132 * rr: 132 * rr + 128, :]), inc=S["GL"], k=16)
        wt("vector", S["GL"])
        for p in range(NPAIR):
            rows = slice(0, 64) if p < 8 else slice(64, 128)
            pc = 64 * (p % 8)
            if r == 0:
                em("vector", lambda E, rr_=rows, pc_=pc, pp_=p, g_=gi:
                   E.tensor_scalar(hrun_()[rr_, pc_: pc_ + 64],
                                   gscr_(g_)[rr_, pc_: pc_ + 64],
                                   wrep_()[rr_, pp_ * 8: pp_ * 8 + 1],
                                   None, ALU.mult))
            else:
                em("vector", lambda E, rr_=rows, pc_=pc, pp_=p, g_=gi, rk=r:
                   E.scalar_tensor_tensor(hrun_()[rr_, pc_: pc_ + 64],
                                          gscr_(g_)[rr_, pc_: pc_ + 64],
                                          wrep_()[rr_, pp_ * 8 + rk: pp_ * 8 + rk + 1],
                                          hrun_()[rr_, pc_: pc_ + 64],
                                          ALU.mult, ALU.add))
        em("vector", lambda E: E.nop(), inc=S["GB"])
        gscr_done[gi] = S["GB"].token()
    for c in range(NCHUNK):
        for p in range(NPAIR):
            rows = slice(0, 64) if p < 8 else slice(64, 128)
            pc = 64 * (p % 8)
            cb = (p % 8) * 8 + c
            em("vector", lambda E, rr=rows, pc_=pc, cb_=cb, pp_=p, cc=c:
               E.scalar_tensor_tensor(y1t_()[rr, 0:64], hrun_()[rr, pc_: pc_ + 64],
                                      lamrep_()[rr, pp_ * 8 + cc: pp_ * 8 + cc + 1],
                                      STB[rr, 64 * cb_: 64 * (cb_ + 1)],
                                      ALU.mult, ALU.add))
            em("vector", lambda E, rr=rows, pc_=pc, cb_=cb:
               E.tensor_copy(STB[rr, 64 * cb_: 64 * (cb_ + 1)],
                             hrun_()[rr, pc_: pc_ + 64]))
            em("vector", lambda E, rr=rows, pc_=pc:
               E.tensor_copy(hrun_()[rr, pc_: pc_ + 64], y1t_()[rr, 0:64]))
    em("vector", lambda E: E.nop(), inc=S["EX"])
    H0_DONE = S["EX"].token()

    barrier()
    checkpoint("p3")
    rP.close()

    r45 = ExitStack()
    pyf = [r45.enter_context(nc.psum_tensor("pyf0", [128, 128])),
           r45.enter_context(nc.psum_tensor("pyf1", [128, 128]))]
    pstatC = r45.enter_context(nc.psum_tensor("pstatC", [1, 512]))
    pstat2C = r45.enter_context(nc.psum_tensor("pstat2C", [1, 512]))
    pbc = r45.enter_context(nc.psum_tensor("pbc", [128, 512]))
    po = r45.enter_context(nc.psum_tensor("po", [128, 512]))

    pyf_guard = [0, 0]
    h0s_guard = [0]

    def y_final_chunk(c):
        wt("vector", S["EX"], H0_DONE)
        if c > 0:
            wt("scalar", S["MC"])
        for m in range(2):
            em("scalar", lambda E, cc=c, mm=m: E.dma_start(
                out=bass.AP(sb["ba"], 32 * mm * BAP, [[BAP, 32], [128, NPAIR], [1, CH]]),
                in_=bass.AP(edram, mm * TOK + cc * CH,
                            [[0, 32], [2 * TOK, NPAIR], [1, CH]])), inc=S["DB2"], k=16, dma=True)
            em("scalar", lambda E, cc=c, mm=m: E.dma_start(
                out=bass.AP(sb["ba"], 32 * mm * BAP + 2048,
                            [[BAP, 32], [128, NPAIR], [1, CH]]),
                in_=bass.AP(cmdram, cc * CH, [[TOK, 32], [0, NPAIR], [1, CH]])),
                inc=S["DB2"], k=16, dma=True)
        wt("vector", S["DB2"])
        if c > 0:
            wt("vector", S["Y2"])
        em("vector", lambda E: E.tensor_tensor(chat_(), EB2_(), CMR_(), ALU.mult),
           inc=S["MC"])
        for p in range(NPAIR):
            yz = pyf[p % 2]
            rows = slice(0, 64) if p < 8 else slice(64, 128)
            cb = (p % 8) * 8 + c
            wt("vector", S["MC"])
            if h0s_guard[0]:
                wt("vector", S["YP"], h0s_guard[0])
            em("vector", lambda E, rr=rows, cb_=cb:
               E.tensor_copy(sb["h0s"][:, :], STB[rr, 64 * cb_: 64 * (cb_ + 1)]),
               inc=S["HC"])
            wt("tensor", S["HC"])
            if pyf_guard[p % 2]:
                wt("tensor", S["Y2"], pyf_guard[p % 2])
            for m in range(2):
                em("tensor", lambda E, pp_=p, mm=m, yz_=yz:
                   E.matmul(yz_[64 * mm: 64 * (mm + 1), :],
                            sb["h0s"][32 * mm: 32 * (mm + 1), :],
                            chat_()[32 * mm: 32 * (mm + 1),
                                    128 * pp_: 128 * (pp_ + 1)],
                            start=True, stop=True),
                   inc=S["YP"])
            h0s_guard[0] = S["YP"].token()
            wt("vector", S["YP"])
            em("vector", lambda E, cc=c, pp_=p, yz_=yz:
               E.tensor_tensor(y1b_()[:, :], yz_[:, :],
                               sb["Y2"][:, pp_ * TOK + cc * CH: pp_ * TOK + (cc + 1) * CH],
                               ALU.add))
            em("vector", lambda E, cc=c, pp_=p:
               E.tensor_tensor(sb["Y2"][:, pp_ * TOK + cc * CH: pp_ * TOK + (cc + 1) * CH],
                               y1b_()[:, :],
                               sb["sg"][:, pp_ * TOK + cc * CH: pp_ * TOK + (cc + 1) * CH],
                               ALU.mult), inc=S["Y2"])
            pyf_guard[p % 2] = S["Y2"].token()

    def tail_half(th):
        toff = NHALF * th
        for j in range(NBLK):
            wt("scalar", S["Y2"])
            if j > 0:
                wt("scalar", S["RM"])
            em("scalar", lambda E, jj=j: E.activation(
                rsq16_(), sb["Y2"][:, jj * TOK + toff: jj * TOK + toff + 512],
                AF.Square), inc=S["SC"])
            wt("tensor", S["SC"])
            em("tensor", lambda E, jj=j:
               E.matmul(pstatC[:, :], sb["onescol"][:, :], rsq16_(),
                        start=(jj == 0), stop=(jj == NBLK - 1)),
               inc=S["RM"])
        wt("vector", S["RM"])
        em("vector", lambda E: E.tensor_copy(stat0_(), pstatC[:, :]), inc=S["RM"])
        wt("scalar", S["RM"])
        em("scalar", lambda E: E.activation(stat0_(), stat0_(), AF.Ln,
                                            scale=1.0 / D, bias=sb["epsb"][0:1, 0:1]))
        em("scalar", lambda E: E.activation(stat0_(), stat0_(), AF.Exp, scale=-0.5),
           inc=S["SC"])
        wt("vector", S["SC"])
        em("vector", lambda E: E.memset(onesrow_(), 1.0), inc=S["RM"])
        wt("tensor", S["RM"])
        em("tensor", lambda E: E.matmul(pbc[:, :], onesrow_(), stat0_(),
                                        start=True, stop=True), inc=S["RM"])
        wt("vector", S["RM"])
        em("vector", lambda E: E.tensor_copy(arms_(), pbc[:, :]), inc=S["RM"])
        for j in range(NBLK):
            em("vector", lambda E, jj=j: E.tensor_tensor(
                sb["Y2"][:, jj * TOK + toff: jj * TOK + toff + 512],
                sb["Y2"][:, jj * TOK + toff: jj * TOK + toff + 512], arms_(),
                ALU.mult))
            em("vector", lambda E, jj=j: E.tensor_scalar(
                sb["Y2"][:, jj * TOK + toff: jj * TOK + toff + 512],
                sb["Y2"][:, jj * TOK + toff: jj * TOK + toff + 512],
                sb["nwb"][:, jj:jj + 1], None, ALU.mult), inc=S["RM"])
        YH = S["RM"].token()
        po_guard = [0]
        wt("tensor", S["FF"])
        for j in range(NBLK):
            buf, lsem, ltgt = stream_weight(Wout_d, j)
            wt("sync", S["WO"])
            em("sync", lambda E, jj=j: E.dma_start(
                out=xr_(), in_=xT_d[128 * jj: 128 * (jj + 1), 3 + toff: 3 + toff + 512]),
                inc=S["XR"], k=16)
            wt("tensor", lsem, ltgt)
            wt("tensor", S["RM"], YH)
            if po_guard[0]:
                wt("tensor", S["WO"], po_guard[0])
            for k in range(KT):
                last = (k == KT - 1)
                em("tensor", lambda E, b=buf, kk=k:
                   E.matmul(po[:, :], b[:, 128 * kk: 128 * (kk + 1)],
                            sb["Y2"][:, kk * TOK + toff: kk * TOK + toff + 512],
                            start=(kk == 0), stop=(kk == KT - 1)),
                   inc=S["YP"] if last else None)
            mark_wfree(S["YP"])
            wt("vector", S["YP"])
            wt("vector", S["XR"])
            em("vector", lambda E, jj=j:
               E.tensor_tensor(rH_()[:, jj * 512: (jj + 1) * 512], po[:, :],
                               xr_(), ALU.add), inc=S["WO"])
            po_guard[0] = S["WO"].token()

        def ln(make16, store_out, donesem):
            for j in range(NBLK):
                wt("vector", S[donesem])
                if j > 0:
                    wt("vector", S["L1"])
                em("vector", lambda E, jj=j: E.tensor_copy(
                    rb16_(), rH_()[:, jj * 512:(jj + 1) * 512]), inc=S["L1"])
                wt("scalar", S["L1"])
                em("scalar", lambda E: E.activation(rsq16_(), rb16_(), AF.Square),
                   inc=S["SC"])
                wt("tensor", S["SC"])
                em("tensor", lambda E, jj=j:
                   E.matmul(pstatC[:, :], sb["onescol"][:, :], rb16_(),
                            start=(jj == 0), stop=(jj == NBLK - 1)),
                   inc=S["L1"])
                em("tensor", lambda E, jj=j:
                   E.matmul(pstat2C[:, :], sb["onescol"][:, :], rsq16_(),
                            start=(jj == 0), stop=(jj == NBLK - 1)),
                   inc=S["L1"])
            wt("vector", S["L1"])
            em("vector", lambda E: E.tensor_scalar(stat0_(), pstatC[:, :], 1.0 / D,
                                                   None, ALU.mult))
            em("vector", lambda E: E.tensor_scalar(stat1_(), pstat2C[:, :], 1.0 / D,
                                                   None, ALU.mult))
            em("vector", lambda E: E.tensor_tensor(stat2_(), stat0_(), stat0_(),
                                                   ALU.mult))
            em("vector", lambda E: E.tensor_tensor(stat1_(), stat1_(), stat2_(),
                                                   ALU.subtract), inc=S["L1"])
            wt("scalar", S["L1"])
            em("scalar", lambda E: E.activation(stat1_(), stat1_(), AF.Ln,
                                                bias=sb["epsb"][0:1, 1:2]))
            em("scalar", lambda E: E.activation(stat1_(), stat1_(), AF.Exp,
                                                scale=-0.5), inc=S["SC"])
            wt("vector", S["SC"])
            em("vector", lambda E: E.tensor_tensor(stat2_(), stat0_(), stat1_(),
                                                   ALU.mult), inc=S["L1"])
            wt("tensor", S["L1"])
            em("tensor", lambda E: E.matmul(pbc[:, :], onesrow_(), stat1_(),
                                            start=True, stop=True), inc=S["L1"])
            wt("vector", S["L1"])
            em("vector", lambda E: E.tensor_copy(Ab_(), pbc[:, :]), inc=S["L1"])
            wt("tensor", S["L1"])
            em("tensor", lambda E: E.matmul(pbc[:, :], onesrow_(), stat2_(),
                                            start=True, stop=True), inc=S["L1"])
            wt("vector", S["L1"])
            em("vector", lambda E: E.tensor_copy(Bb_(), pbc[:, :]), inc=S["L1"])
            for j in range(NBLK):
                if store_out:
                    em("vector", lambda E, jj=j: E.tensor_tensor(
                        ost_(), rH_()[:, jj * 512:(jj + 1) * 512], Ab_(), ALU.mult))
                    em("vector", lambda E, jj=j: E.tensor_tensor(
                        ost_(), ost_(), Bb_(), ALU.subtract), inc=S["OD"])
                    wt("sync", S["OD"])
                    em("sync", lambda E, jj=j: E.dma_start(
                        out=out_d[128 * jj: 128 * (jj + 1), toff: toff + 512],
                        in_=ost_()), inc=S["OD"], k=16)
                    wt("vector", S["OD"])
                else:
                    em("vector", lambda E, jj=j: E.tensor_tensor(
                        rH_()[:, jj * 512:(jj + 1) * 512],
                        rH_()[:, jj * 512:(jj + 1) * 512], Ab_(), ALU.mult))
                    em("vector", lambda E, jj=j: E.tensor_tensor(
                        rH_()[:, jj * 512:(jj + 1) * 512],
                        rH_()[:, jj * 512:(jj + 1) * 512], Bb_(), ALU.subtract),
                        inc=S["L1"])
                    if make16:
                        em("vector", lambda E, jj=j: E.tensor_copy(
                            r16_()[:, jj * 512:(jj + 1) * 512],
                            rH_()[:, jj * 512:(jj + 1) * 512]), inc=S["L1"])

        ln(make16=True, store_out=False, donesem="WO")
        po_guard2 = [0]
        for j in range(NBLK):
            buf, lsem, ltgt = stream_weight(WffT_d, j)
            wt("tensor", lsem, ltgt)
            wt("tensor", S["L1"])
            if po_guard2[0]:
                wt("tensor", S["FF"], po_guard2[0])
            for k in range(KT):
                last = (k == KT - 1)
                em("tensor", lambda E, b=buf, kk=k:
                   E.matmul(po[:, :], b[:, 128 * kk: 128 * (kk + 1)],
                            r16_()[:, kk * 512: (kk + 1) * 512],
                            start=(kk == 0), stop=(kk == KT - 1)),
                   inc=S["YP"] if last else None)
            mark_wfree(S["YP"])
            wt("vector", S["YP"])
            em("vector", lambda E, jj=j:
               E.tensor_scalar(ost_(), po[:, :], sb["fsb"][:, jj:jj + 1],
                               sb["fbb"][:, jj:jj + 1], ALU.mult, ALU.add))
            em("vector", lambda E, jj=j:
               E.tensor_tensor(rH_()[:, jj * 512:(jj + 1) * 512],
                               rH_()[:, jj * 512:(jj + 1) * 512], ost_(), ALU.add),
               inc=S["FF"])
            po_guard2[0] = S["FF"].token()
        ln(make16=False, store_out=True, donesem="FF")

    for c in range(NCHUNK):
        y_final_chunk(c)
        checkpoint("y%d" % c)
    tail_half(0)
    checkpoint("t0")
    tail_half(1)

    wt("sync", S["OD"])
    r45.close()
    ctx.close()

    with nc.Block() as block:
        @block.sync
        def _(E):
            for fn in ops["sync"]:
                fn(E)

        @block.gpsimd
        def _(E):
            for fn in ops["gpsimd"]:
                fn(E)

        @block.tensor
        def _(E):
            for fn in ops["tensor"]:
                fn(E)

        @block.vector
        def _(E):
            for fn in ops["vector"]:
                fn(E)

        @block.scalar
        def _(E):
            for fn in ops["scalar"]:
                fn(E)

    return nc


_NC_CACHE = {}


def _host_inputs(inputs):
    f32 = np.float32
    x = np.asarray(inputs["L0_attn_x"], f32)
    Win0 = np.asarray(inputs["L0_attn_W_in"], f32)
    Win = np.zeros((D, 4224), f32)
    Win[:, :4192] = Win0
    b_in = np.asarray(inputs["L0_attn_b_in"], f32)
    conv_w = np.asarray(inputs["L0_attn_conv_w"], f32)
    conv_b = np.asarray(inputs["L0_attn_conv_b"], f32)
    A_log = np.asarray(inputs["L0_attn_A_log"], f32)
    dt_bias = np.asarray(inputs["L0_attn_dt_bias"], f32)
    D_param = np.asarray(inputs["L0_attn_D_param"], f32)
    norm_w = np.asarray(inputs["L0_attn_norm_w"], f32)
    Wout = np.ascontiguousarray(np.asarray(inputs["L0_attn_W_out"], f32))
    ffw = np.asarray(inputs["L0_ffn_weight"], f32)
    ffs = np.asarray(inputs["L0_ffn_scale_weight"], f32)
    ffb = np.asarray(inputs["L0_ffn_bias"], f32)
    WffT = np.ascontiguousarray(ffw.T)

    b_in_r = np.zeros((128, 33), f32)
    for j in range(33):
        nreal = 96 if j == 32 else 128
        b_in_r[:nreal, j] = b_in[128 * j: 128 * j + nreal]
    cwr = np.zeros((128, 68), f32)
    cbr = np.zeros((128, 17), f32)
    for j in range(17):
        nreal = 64 if j == 16 else 128
        for k in range(KCONV):
            cwr[:nreal, 4 * j + k] = conv_w[128 * j: 128 * j + nreal, k]
        cbr[:nreal, j] = conv_b[128 * j: 128 * j + nreal]
    Dcol = np.zeros((128, NPAIR), f32)
    for p in range(NPAIR):
        Dcol[0:64, p] = D_param[2 * p]
        Dcol[64:128, p] = D_param[2 * p + 1]
    nwcol = norm_w.reshape(NBLK, 128).T.astype(f32).copy()
    fscol = ffs.reshape(NBLK, 128).T.astype(f32).copy()
    fbcol = ffb.reshape(NBLK, 128).T.astype(f32).copy()
    tri = np.zeros((128, 128), f32)
    for tp in range(128):
        tri[tp, :tp] = -1e30

    shared = dict(
        Win=Win, Wout=Wout, WffT=WffT, b_in_r=b_in_r, cwr=cwr, cbr=cbr,
        dtb32=dt_bias.reshape(32, 1).astype(f32),
        alog32=A_log.reshape(32, 1).astype(f32),
        Dcol=Dcol, nwcol=nwcol, fscol=fscol, fbcol=fbcol,
        identf_d=np.eye(128, dtype=f32),
        tribias_d=tri,
        epsb_d=np.array([[EPS_M, EPS_LN]], dtype=f32),
    )

    in_maps = []
    for c in range(8):
        b, q = c // 4, c % 4
        t0 = q * TOK
        if q == 0:
            xs_ = np.concatenate([np.zeros((3, D), f32), x[b, 0:TOK]], axis=0)
        else:
            xs_ = x[b, t0 - 3: t0 + TOK]
        xTc = np.ascontiguousarray(xs_.T)
        hflag = np.full((128, 1), 0.0 if q == 0 else 1.0, f32)
        ind8 = np.zeros((128, 8), f32)
        ind8[:, c] = 1.0
        selm8 = np.zeros((8, 8), f32)
        pmask8 = np.zeros((8, 1), f32)
        for g in range(8):
            if g // 4 == b and g % 4 < q:
                pmask8[g, 0] = 1.0
            for gp in range(8):
                if gp // 4 == b and g // 4 == b and (g % 4) < (gp % 4) < q:
                    selm8[gp, g] = 1.0
        m = dict(shared)
        m.update(xT=xTc, hflag=hflag, ind8=ind8, selm8=selm8, pmask8=pmask8)
        in_maps.append(m)
    return in_maps


def kernel(**inputs):
    in_maps = _host_inputs(inputs)
    if "nc" not in _NC_CACHE:
        _NC_CACHE["nc"] = build_nc()
    nc = _NC_CACHE["nc"]
    res = run_bass_kernel_spmd(nc, in_maps, core_ids=list(range(8)))
    out = np.zeros((B, L, D), np.float32)
    for c in range(8):
        b, q = c // 4, c % 4
        out[b, q * TOK:(q + 1) * TOK, :] = np.asarray(res.results[c]["out"]).T
    return out


# version 06
# speedup vs baseline: 1.1741x; 1.1741x over previous
import numpy as np
import ml_dtypes
from contextlib import ExitStack

import concourse.bass as bass
import concourse.mybir as mybir
from concourse.bass_utils import run_bass_kernel_spmd

F32 = mybir.dt.float32
BF16 = mybir.dt.bfloat16
AF = mybir.ActivationFunctionType
ALU = mybir.AluOpType
BF = ml_dtypes.bfloat16

B, L, D = 2, 4096, 2048
KCONV = 4
TOK = 1024
NHALF = 512
NCHUNK = 8
CH = 128
EPS_M = 1e-6
EPS_LN = 1e-5
NBLK = 16
NPAIR = 16
KT = 16
HBW = NHALF + 4
XPITCH = TOK + 3

ENGINES = ("sync", "gpsimd", "tensor", "vector", "scalar")


TRACE = None
STOP = None


class Sem:

    def __init__(self, handle, name="?"):
        self.h = handle
        self.n = 0
        self.real_n = 0
        self.marks = {}
        self.name = name

    def inc(self, k=1):
        self.n += k
        return self.n

    def token(self):
        return (self.real_n, dict(self.marks))


def build_nc():
    nc = bass.Bass("TRN2", target_bir_lowering=False, num_devices=8)
    ctx = ExitStack()
    EC = ctx.enter_context

    def din(name, shape, dt=F32):
        return nc.dram_tensor(name, shape, dt, kind="ExternalInput")

    xT_d = din("xT", [D, XPITCH])
    Win_d = din("Win", [D, 4224])
    Wout_d = din("Wout", [D, D])
    WffT_d = din("WffT", [D, D])
    b_in_r_d = din("b_in_r", [128, 33])
    cwr_d = din("cwr", [128, 17 * KCONV])
    cbr_d = din("cbr", [128, 17])
    dtb32_d = din("dtb32", [32, 1])
    alog32_d = din("alog32", [32, 1])
    Dcol_d = din("Dcol", [128, NPAIR])
    nw_d = din("nwcol", [128, NBLK])
    fs_d = din("fscol", [128, NBLK])
    fb_d = din("fbcol", [128, NBLK])
    hflag_d = din("hflag", [128, 1])
    ind8_d = din("ind8", [128, 8])
    selm8_d = din("selm8", [8, 8])
    pmask8_d = din("pmask8", [8, 1])
    identf_d = din("identf_d", [128, 128])
    tribias_d = din("tribias_d", [128, 128])
    epsb_d = din("epsb_d", [1, 2])

    out_d = nc.dram_tensor("out", [D, TOK], F32, kind="ExternalOutput")

    ccb = nc.dram_tensor("ccb", [8 * 132, 512], F32)
    shid = nc.dram_tensor("shid", [32, TOK], BF16)
    slod = nc.dram_tensor("slod", [32, TOK], BF16)
    edram = nc.dram_tensor("edram", [32, TOK], BF16)
    cmdram = nc.dram_tensor("cmdram", [32, TOK], BF16)
    e2dram = nc.dram_tensor("e2dram", [32, 8], F32)
    lamdram = nc.dram_tensor("lamdram", [32, 8], F32)
    wdram = nc.dram_tensor("wdram", [32, 8], F32)

    S = {}
    for s in ("CL", "WL", "EV", "CV", "SC", "TR", "XW", "ST", "YP", "MC",
              "EX", "CC", "CCD", "RB", "GA", "GL", "SB", "SBG", "DB", "DB2", "Y2", "RM", "WO", "XR",
              "L1", "FF", "OD", "BAR", "HC", "GB"):
        S[s] = Sem(EC(nc.semaphore("s" + s)), s)
    wload = [Sem(EC(nc.semaphore("wl0")), "wl0"), Sem(EC(nc.semaphore("wl1")), "wl1")]
    wfree = [Sem(EC(nc.semaphore("wf0")), "wf0"), Sem(EC(nc.semaphore("wf1")), "wf1")]

    sb = {}

    def alloc(name, shape, dt=F32):
        sb[name] = EC(nc.sbuf_tensor(name, shape, dt))
        return sb[name]

    for nm, shp, dt in [
        ("cbir", [128, 33], F32), ("cwrb", [128, 68], F32), ("cbrb", [128, 17], F32),
        ("dtbb", [32, 1], F32), ("alogb", [32, 1], F32), ("negA3", [96, 1], F32),
        ("Dcolb", [128, NPAIR], F32), ("nwb", [128, NBLK], F32),
        ("fsb", [128, NBLK], F32), ("fbb", [128, NBLK], F32),
        ("hflagb", [128, 1], F32), ("ind8b", [128, 8], F32),
        ("selm8b", [8, 8], F32), ("pmask8b", [8, 1], F32),
        ("identb", [128, 128], BF16), ("identf", [128, 128], F32),
        ("ones2b", [32, 128], BF16), ("onescol", [128, 1], BF16),
        ("tribias", [128, 128], F32), ("epsb", [1, 2], F32),
        ("xtb", [128, KT * XPITCH], BF16),
        ("hb", [128, 17 * HBW], F32),
        ("fa", [128, 6976], F32),
        ("ba", [128, 7168], BF16),
        ("sttb", [128, 4096], BF16),
        ("sg", [128, NBLK * TOK], BF16),
        ("Y2", [128, NBLK * TOK], BF16),
        ("xsc", [128, 2048], BF16),
        ("bcxc", [64, 132], BF16),
        ("cm0c", [32, 128], BF16),
        ("bctc", [128, 64], BF16),
        ("wsTc", [128, 96], F32),
        ("smlc", [128, 32], F32),
        ("s_hic", [32, 128], BF16), ("s_loc", [32, 128], BF16),
        ("h0s", [64, 64], BF16),
        ("wb0", [128, KT * 128], BF16), ("wb1", [128, KT * 128], BF16),
        ("lamlog", [32, 8], F32), ("lamcum", [32, 8], F32), ("E2c", [32, 8], F32),
        ("lamc", [32, 8], F32), ("lamT", [1, 32], F32), ("lmw", [1, 32], F32),
        ("glamT", [8, 32], F32), ("wcomb", [8, 32], F32),
    ]:
        alloc(nm, shp, dt)

    FA = sb["fa"]
    FAP = 6976
    hrun_ = lambda: FA[:, 0:512]
    scc_ = lambda: FA[:, 512:1024]
    mskw_ = lambda: FA[:, 1024:1536]
    gscr_ = lambda i: FA[:, 1536 + 512 * i: 2048 + 512 * i]
    y1t_ = lambda: FA[:, 2560:2624]
    arms_ = lambda: FA[:, 0:512]
    Ab_ = lambda: FA[:, 512:1024]
    Bb_ = lambda: FA[:, 1024:1536]
    xr_ = lambda: FA[:, 1536:2048]
    ost_ = lambda: FA[:, 2048:2560]
    E2rep_ = lambda: FA[:, 2624:2752]
    lamrep_ = lambda: FA[:, 2752:2880]
    wrep_ = lambda: FA[:, 2880:3008]
    stat0_ = lambda: FA[0:1, 3072:3584]
    stat1_ = lambda: FA[0:1, 3584:4096]
    stat2_ = lambda: FA[0:1, 4096:4608]
    onesrow_ = lambda: FA[0:1, 4608:4736]
    y1b_ = lambda: FA[:, 4736:4864]
    wsstk_ = lambda: FA[0:96, 4864:4992]
    dargb_ = lambda: FA[:, 4992:5504]
    swA_ = lambda: FA[32:64, 5504:5696]
    swB_ = lambda: FA[0:32, 5504:5696]
    dtraw_ = lambda: FA[0:32, 5696:5824]
    srep_ = lambda: FA[32:64, 5696:5824]
    dtb_ = lambda: FA[64:96, 5696:5824]
    eS_ = lambda: FA[0:32, 5824:5952]
    lndt_ = lambda: FA[32:64, 5824:5952]
    wtil_ = lambda: FA[64:96, 5824:5952]
    conva_ = lambda: FA[:, 5952:6464]
    convb_ = lambda: FA[:, 6464:6976]

    BA = sb["ba"]
    BAP = 7168
    sfhl_ = lambda: BA[0:2, 0:4096]
    sfhl32_ = lambda: BA[0:32, 0:4096]
    XTc_ = lambda: BA[:, 4096:6144]
    gbuf_ = lambda: BA[:, 6144:6272]
    dexpb_ = lambda: BA[:, 6272:6784]
    mbuf_ = lambda i: BA[:, 6784 + 128 * i: 6912 + 128 * i]
    xtw_ = lambda: BA[:, 7040:7168]
    EB2_ = lambda: BA[0:64, 0:2048]
    CMR_ = lambda: BA[0:64, 2048:4096]
    chat_ = lambda: BA[0:64, 4096:6144]
    rb16_ = lambda: BA[:, 0:512]
    rsq16_ = lambda: BA[:, 512:1024]

    r16_ = lambda: sb["xtb"][:, 0:8192]
    rH_ = lambda: sb["hb"][:, 0:8192]
    STB = sb["sttb"]

    ops = {e: [] for e in ENGINES}
    stopped = [False]

    def checkpoint(label):
        if STOP == label:
            stopped[0] = True

    SSELF = {e: Sem(EC(nc.semaphore("self_" + e)), "self_" + e)
             for e in ("vector", "scalar", "gpsimd")}

    tpace = [0]

    def em(eng, fn, inc=None, k=1, inc2=None, k2=1, dma=False):
        if stopped[0]:
            return
        selfs = SSELF.get(eng) if not dma else None
        if selfs is not None:
            if selfs.n > 0:
                vv = selfs.n
                if TRACE is not None:
                    TRACE[eng].append(("wait", selfs.name, vv))
                ops[eng].append(lambda E, s=selfs, v=vv: E.wait_ge(s.h, v))
            for s_, k_ in ((inc, k), (inc2, k2)):
                if s_ is not None:
                    s_.inc(k_)
                    s_.marks[eng] = selfs.n + 1
            selfs.inc(1)
            if TRACE is not None:
                TRACE[eng].append(("inc", selfs.name, 1))

            def wrapped(E, fn=fn, s=selfs):
                fn(E).then_inc(s.h, 1)
            ops[eng].append(wrapped)
            return
        assert inc2 is None or inc is None, "two real incs on one instruction"
        real = inc if inc is not None else inc2
        rk = k if inc is not None else k2
        if real is not None:
            real.inc(rk)
            real.real_n += rk
            if TRACE is not None:
                TRACE[eng].append(("inc", real.name, rk))

            def wrapped(E, fn=fn, s=real, kk=rk):
                fn(E).then_inc(s.h, kk)
            ops[eng].append(wrapped)
        else:
            ops[eng].append(fn)

    def _emit_wait(eng, sem, v):
        if v > 0:
            if TRACE is not None:
                TRACE[eng].append(("wait", sem.name, v))
            ops[eng].append(lambda E, s=sem, vv=v: E.wait_ge(s.h, vv))

    def wt(eng, sem, val=None):
        if stopped[0]:
            return
        if isinstance(val, tuple):
            rn, marks = val
        elif val is None:
            rn, marks = sem.real_n, sem.marks
        else:
            rn, marks = val, {}
        _emit_wait(eng, sem, rn)
        for e, m in marks.items():
            if eng == e:
                continue
            _emit_wait(eng, SSELF[e], m)

    def barrier():
        if stopped[0]:
            return
        S["BAR"].inc(5)
        tgt = S["BAR"].n
        for e in ENGINES:
            if TRACE is not None:
                TRACE[e].append(("inc", "BAR", 1))
                TRACE[e].append(("wait", "BAR", tgt))
            ops[e].append(lambda E: E.sem_inc(S["BAR"].h, 1))
            ops[e].append(lambda E, t=tgt: E.wait_ge(S["BAR"].h, t))

    for dst, src in [("cbir", b_in_r_d), ("cwrb", cwr_d), ("cbrb", cbr_d),
                     ("dtbb", dtb32_d), ("alogb", alog32_d), ("Dcolb", Dcol_d),
                     ("nwb", nw_d), ("fsb", fs_d), ("fbb", fb_d),
                     ("hflagb", hflag_d), ("ind8b", ind8_d), ("selm8b", selm8_d),
                     ("pmask8b", pmask8_d),
                     ("identf", identf_d),
                     ("tribias", tribias_d), ("epsb", epsb_d)]:
        em("sync", lambda E, d=sb[dst], s=src: E.dma_start(out=d[:, :], in_=s[:, :]),
           inc=S["CL"], k=16)

    def ld_xtb(E):
        src = bass.AP(xT_d, 0, [[XPITCH, 128], [128 * XPITCH, KT], [1, XPITCH]])
        dst = bass.AP(sb["xtb"], 0, [[KT * XPITCH, 128], [XPITCH, KT], [1, XPITCH]])
        return E.dma_start(out=dst, in_=src)
    em("gpsimd", lambda E: E.memset(sb["ba"][0:32, 0:4096], 0.0))
    em("gpsimd", ld_xtb, inc=S["WL"], k=16, dma=True)

    wt("scalar", S["CL"])
    em("scalar", lambda E: E.activation(sb["negA3"][0:32, :], sb["alogb"][:, :],
                                        AF.Exp), inc=S["SC"])
    wt("vector", S["CL"])
    em("vector", lambda E: E.tensor_copy(sb["identb"][:, :], sb["identf"][:, :]))
    em("vector", lambda E: E.memset(sb["ones2b"][:, :], 1.0))
    em("vector", lambda E: E.memset(sb["onescol"][:, :], 1.0), inc=S["CV"])
    wt("vector", S["SC"])
    em("vector", lambda E: E.tensor_scalar_mul(sb["negA3"][0:32, :],
                                               sb["negA3"][0:32, :], -1.0))
    em("vector", lambda E: E.tensor_copy(sb["negA3"][64:96, :], sb["negA3"][0:32, :]),
       inc=S["CV"])
    wt("vector", S["CL"])
    em("vector", lambda E: E.memset(FA[0:64, 5504:5696], 0.0), inc=S["CV"])

    checkpoint("s0")
    wl_count = [0]
    wfree_tok = [None, None]

    def stream_weight(wdram_t, otile):
        i = wl_count[0] % 2
        buf = sb["wb0"] if i == 0 else sb["wb1"]
        if wl_count[0] >= 2 and wfree_tok[i] is not None:
            wt("gpsimd", wfree_tok[i][0], wfree_tok[i][1])

        def ld(E, w=wdram_t, o=otile, b=buf):
            src = bass.AP(w, o * 128, [[w.shape[1], 128], [128 * w.shape[1], KT], [1, 128]])
            dst = bass.AP(b, 0, [[KT * 128, 128], [128, KT], [1, 128]])
            return E.dma_start(out=dst, in_=src)
        em("gpsimd", ld, inc=wload[i], k=16, dma=True)
        wl_count[0] += 1
        return buf, wload[i], wload[i].n

    def mark_wfree(sem):
        i = (wl_count[0] - 1) % 2
        wfree_tok[i] = (sem, sem.token())

    rP = ExitStack()
    pp = [rP.enter_context(nc.psum_tensor("pp0", [128, 512])),
          rP.enter_context(nc.psum_tensor("pp1", [128, 512]))]
    ptr = rP.enter_context(nc.psum_tensor("ptr", [128, 128], BF16))
    ptrf = rP.enter_context(nc.psum_tensor("ptrf", [128, 128]))
    psb = [rP.enter_context(nc.psum_tensor("psb0", [128, 512])),
           rP.enter_context(nc.psum_tensor("psb1", [128, 512]))]
    py = [rP.enter_context(nc.psum_tensor("py0", [128, 128])),
          rP.enter_context(nc.psum_tensor("py1", [128, 128]))]

    pp_guard = [0, 0]
    halo_guard = [0]
    ptr_guard = [0]
    ptrf_guard = [0]
    psb_guard = [0, 0]
    py_guard = [None, None]

    wt("tensor", S["CL"])
    wt("tensor", S["WL"], 16)

    def proj_half(half):
        for j in range(17):
            ot = 16 + j
            buf, lsem, ltgt = stream_weight(Win_d, ot)
            p = pp[j % 2]
            wt("tensor", lsem, ltgt)
            if pp_guard[j % 2]:
                wt("tensor", S["EV"], pp_guard[j % 2])
            for k in range(KT):
                last = (k == KT - 1)
                em("tensor", lambda E, b=buf, kk=k, p_=p, h=half:
                   E.matmul(p_[:, 0:512], b[:, 128 * kk:128 * (kk + 1)],
                            sb["xtb"][:, kk * XPITCH + 3 + NHALF * h:
                                      kk * XPITCH + 3 + NHALF * h + 512],
                            start=(kk == 0), stop=(kk == KT - 1)),
                   inc=S["YP"] if last else None)
            if half == 0:
                if halo_guard[0]:
                    wt("tensor", S["EV"], halo_guard[0])
                for k in range(KT):
                    last = (k == KT - 1)
                    em("tensor", lambda E, b=buf, kk=k:
                       E.matmul(ptrf[:, 96:99], b[:, 128 * kk:128 * (kk + 1)],
                                sb["xtb"][:, kk * XPITCH: kk * XPITCH + 3],
                                start=(kk == 0), stop=(kk == KT - 1)),
                       inc=S["YP"] if last else None)
            mark_wfree(S["YP"])
            wt("vector", S["YP"])
            em("vector", lambda E, jj=j, p_=p, ot_=ot:
               E.tensor_scalar(sb["hb"][:, jj * HBW + 4: jj * HBW + 516], p_[:, 0:512],
                               sb["cbir"][:, ot_:ot_ + 1], None, ALU.add), inc=S["EV"])
            pp_guard[j % 2] = S["EV"].token()
            if half == 0:
                em("vector", lambda E, jj=j, ot_=ot:
                   E.tensor_scalar(sb["hb"][:, jj * HBW + 1: jj * HBW + 4],
                                   ptrf[:, 96:99],
                                   sb["cbir"][:, ot_:ot_ + 1], sb["hflagb"][:, 0:1],
                                   ALU.add, ALU.mult), inc=S["EV"])
                halo_guard[0] = S["EV"].token()
            checkpoint("pj%d_%d" % (half, j))

    def chunk_pipeline(c):
        cl = c % 4
        for j in range(17):
            rows = 128 if j < 16 else 64
            base = j * HBW
            acc = conva_() if j % 2 == 0 else convb_()
            wt("vector", S["EV"])
            if j >= 2:
                wt("vector", S["SC"])
            em("vector", lambda E, jj=j, r=rows, b=base, a=acc, w0=1 + cl * CH:
               E.tensor_scalar(a[0:r, 0:CH], sb["hb"][0:r, b + w0: b + w0 + CH],
                               sb["cwrb"][0:r, 4 * jj: 4 * jj + 1],
                               sb["cbrb"][0:r, jj: jj + 1], ALU.mult, ALU.add))
            for k in range(1, 4):
                em("vector", lambda E, jj=j, r=rows, b=base, kk=k, a=acc, w0=1 + cl * CH:
                   E.scalar_tensor_tensor(a[0:r, 0:CH],
                                          sb["hb"][0:r, b + w0 + kk: b + w0 + kk + CH],
                                          sb["cwrb"][0:r, 4 * jj + kk: 4 * jj + kk + 1],
                                          a[0:r, 0:CH], ALU.mult, ALU.add),
                   inc=S["CV"] if k == 3 else None)
            wt("scalar", S["CV"])
            if j < 16:
                wt("scalar", S["Y2"])
                em("scalar", lambda E, jj=j, a=acc:
                   E.activation(sb["xsc"][:, jj * CH:(jj + 1) * CH], a[:, 0:CH],
                                AF.Silu), inc=S["SC"])
            else:
                em("scalar", lambda E, a=acc:
                   E.activation(sb["bcxc"][:, 0:CH], a[0:64, 0:CH], AF.Silu),
                   inc=S["SC"])
        checkpoint("cv%d" % c)
        wt("vector", S["SC"])
        wt("vector", S["SB"])
        wt("vector", S["SBG"])
        em("vector", lambda E, w0=4 + cl * CH:
           E.tensor_copy(dtraw_(), sb["hb"][64:96, 16 * HBW + w0: 16 * HBW + w0 + CH]),
           inc=S["CV"])
        wt("scalar", S["CV"])
        em("scalar", lambda E: E.activation(dtb_(), dtraw_(), AF.Exp,
                                            bias=sb["dtbb"][:, 0:1]))
        em("scalar", lambda E: E.activation(dtb_(), dtb_(), AF.Ln, bias=1.0))
        em("scalar", lambda E: E.activation(lndt_(), dtb_(), AF.Ln), inc=S["SC"])
        wt("vector", S["SC"])
        em("vector", lambda E: E.tensor_scalar(FA[32:64, 5504 + 64:5504 + 192],
                                               dtb_(), sb["negA3"][64:96, 0:1],
                                               None, ALU.mult))
        parity = 0
        for k in [1, 2, 4, 8, 16, 32, 64]:
            srcb = swA_() if parity == 0 else swB_()
            dstb = swB_() if parity == 0 else swA_()
            srow = 32 if parity == 0 else 0
            em("vector", lambda E, sr=srow, kk=k:
               E.tensor_tensor(FA[32 - sr:64 - sr, 5504 + 64:5504 + 192],
                               FA[sr:sr + 32, 5504 + 64:5504 + 192],
                               FA[sr:sr + 32, 5504 + 64 - kk:5504 + 192 - kk],
                               ALU.add))
            parity ^= 1
        em("vector", lambda E: E.tensor_copy(sb["s_hic"][:, :],
                                             FA[0:32, 5504 + 64:5504 + 192]))
        em("vector", lambda E: E.tensor_tensor(sb["s_loc"][:, :],
                                               FA[0:32, 5504 + 64:5504 + 192],
                                               sb["s_hic"][:, :], ALU.subtract))
        em("vector", lambda E: E.tensor_tensor(srep_(), sb["s_hic"][:, :],
                                               sb["s_loc"][:, :], ALU.add))
        em("vector", lambda E: E.tensor_scalar(FA[64:96, 5824:5952], srep_(),
                                               srep_()[:, CH - 1: CH], None,
                                               ALU.subtract))
        em("vector", lambda E, cc=c: E.tensor_copy(sb["lamlog"][:, cc:cc + 1],
                                                   srep_()[:, CH - 1: CH]))
        em("vector", lambda E: E.tensor_copy(sb["cm0c"][:, :],
                                             sb["bcxc"][32:64, 0:CH]), inc=S["CV"])
        wt("scalar", S["CV"])
        em("scalar", lambda E: E.activation(eS_(), srep_(), AF.Exp))
        em("scalar", lambda E: E.activation(wtil_(), wtil_(), AF.Exp, scale=-1.0),
           inc=S["SC"])
        wt("vector", S["SC"])
        em("vector", lambda E: E.tensor_tensor(wtil_(), wtil_(), dtb_(), ALU.mult),
           inc=S["CV"])
        wt("sync", S["CV"])
        em("sync", lambda E, cc=c: E.dma_start(out=shid[:, cc * CH:(cc + 1) * CH],
                                               in_=sb["s_hic"][:, :]), inc=S["SB"], k=16)
        em("sync", lambda E, cc=c: E.dma_start(out=slod[:, cc * CH:(cc + 1) * CH],
                                               in_=sb["s_loc"][:, :]), inc=S["SB"], k=16)
        wt("gpsimd", S["SC"])
        em("gpsimd", lambda E, cc=c: E.dma_start(out=edram[:, cc * CH:(cc + 1) * CH],
                                                 in_=eS_()), inc=S["SBG"], k=16, dma=True)
        em("sync", lambda E, cc=c: E.dma_start(out=cmdram[:, cc * CH:(cc + 1) * CH],
                                               in_=sb["cm0c"][:, :]), inc=S["SB"], k=16)
        checkpoint("dt%d" % c)
        wt("vector", S["CV"])
        if c > 0:
            wt("vector", S["MC"])
        em("vector", lambda E: E.tensor_copy(wsstk_()[0:32, :], srep_()))
        em("vector", lambda E: E.tensor_copy(wsstk_()[32:64, :], wtil_()))
        em("vector", lambda E: E.tensor_copy(wsstk_()[64:96, :], lndt_()),
           inc=S["XW"])
        wt("tensor", S["XW"])
        if ptrf_guard[0]:
            wt("tensor", S["MC"], ptrf_guard[0])
        wt("tensor", S["EV"])
        em("tensor", lambda E: E.transpose(ptrf[:, 0:96], wsstk_()[:, :],
                                           sb["identf"][0:96, 0:96]), inc=S["YP"])
        wt("vector", S["YP"])
        em("vector", lambda E: E.tensor_copy(sb["wsTc"][:, :], ptrf[:, 0:96]),
           inc=S["MC"])
        ptrf_guard[0] = S["MC"].token()
        em("vector", lambda E: E.tensor_tensor(sb["smlc"][:, :], sb["wsTc"][:, 0:32],
                                               sb["wsTc"][:, 64:96], ALU.subtract),
           inc=S["MC"])
        wt("tensor", S["MC"])
        if ptr_guard[0]:
            wt("tensor", S["TR"], ptr_guard[0])
        em("tensor", lambda E: E.transpose(ptr[:, 0:64], sb["bcxc"][:, 0:CH],
                                           sb["identb"][0:64, 0:64]), inc=S["YP"])
        wt("vector", S["YP"])
        em("vector", lambda E: E.tensor_copy(sb["bctc"][:, :], ptr[:, 0:64]),
           inc=S["TR"])
        ptr_guard[0] = S["TR"].token()
        for j in range(NBLK):
            wt("tensor", S["TR"], ptr_guard[0])
            em("tensor", lambda E, jj=j:
               E.transpose(ptr[:, :], sb["xsc"][:, jj * CH:(jj + 1) * CH],
                           sb["identb"][:, :]), inc=S["YP"])
            wt("vector", S["YP"])
            em("vector", lambda E, jj=j:
               E.tensor_copy(XTc_()[:, jj * 128:(jj + 1) * 128], ptr[:, :]),
               inc=S["TR"])
            ptr_guard[0] = S["TR"].token()
        checkpoint("tp%d" % c)
        for p in range(NPAIR):
            wt("vector", S["TR"])
            wt("vector", S["YP"])
            for m in range(2):
                hh = 2 * p + m
                em("vector", lambda E, pp_=p, mm=m, h_=hh:
                   E.tensor_scalar(xtw_()[:, 64 * mm: 64 * (mm + 1)],
                                   XTc_()[:, 128 * pp_ + 64 * mm:
                                          128 * pp_ + 64 * (mm + 1)],
                                   sb["wsTc"][:, 32 + h_: 33 + h_], None, ALU.mult),
                   inc=S["XW"] if m == 1 else None)
            pz = py[p % 2]
            wt("tensor", S["XW"])
            if py_guard[p % 2]:
                gsem, gval = py_guard[p % 2]
                wt("tensor", gsem, gval)
            for m in range(2):
                em("tensor", lambda E, mm=m, pz_=pz:
                   E.matmul(pz_[32 * mm: 32 * (mm + 1), 0:64], sb["bctc"][:, 0:32],
                            xtw_()[:, 64 * mm: 64 * (mm + 1)],
                            start=True, stop=True),
                   inc=S["YP"])
            rows = slice(0, 64) if p < 8 else slice(64, 128)
            cb = (p % 8) * 8 + c
            wt("vector", S["YP"])
            em("vector", lambda E, pz_=pz, rr=rows, cb_=cb:
               E.tensor_copy(STB[rr, 64 * cb_: 64 * (cb_ + 1)], pz_[0:64, 0:64]),
               inc=S["ST"])
            py_guard[p % 2] = (S["ST"], S["ST"].token())
        checkpoint("st%d" % c)
        wt("tensor", S["CV"])
        if ptrf_guard[0]:
            wt("tensor", S["MC"], ptrf_guard[0])
        em("tensor", lambda E: E.matmul(ptrf[:, :], sb["bcxc"][0:32, 0:CH],
                                        sb["cm0c"][:, :], start=True, stop=True),
           inc=S["YP"])
        wt("vector", S["YP"])
        em("vector", lambda E: E.tensor_copy(gbuf_(), ptrf[:, :]), inc=S["MC"])
        ptrf_guard[0] = S["MC"].token()
        checkpoint("g%d" % c)
        if c == 0:
            wt("gpsimd", SSELF["gpsimd"], 1)
        wt("gpsimd", S["SB"])
        wt("gpsimd", S["SBG"])
        if c > 0:
            wt("gpsimd", S["MC"])
        em("gpsimd", lambda E, cc=c: E.dma_start(
            out=sfhl_()[0:1, :],
            in_=bass.AP(shid, cc * CH, [[1, 1], [TOK, 32], [1, CH]])), inc=S["DB"], k=16, dma=True)
        em("gpsimd", lambda E, cc=c: E.dma_start(
            out=sfhl_()[1:2, :],
            in_=bass.AP(slod, cc * CH, [[1, 1], [TOK, 32], [1, CH]])), inc=S["DB"], k=16, dma=True)
        checkpoint("fl%d" % c)
        for g in range(8):
            pz = psb[g % 2]
            wt("tensor", S["DB"])
            if psb_guard[g % 2]:
                wt("tensor", S["MC"], psb_guard[g % 2])
            em("tensor", lambda E, gg=g, pz_=pz:
               E.matmul(pz_[:, 0:512], sb["ones2b"][:, :],
                        sfhl32_()[:, 512 * gg: 512 * (gg + 1)],
                        start=True, stop=True), inc=S["YP"])
            if g == 0:
                checkpoint("sb%d" % c)
            wt("vector", S["YP"])
            wt("vector", S["MC"])
            for hq in range(4):
                hh = 4 * g + hq
                em("vector", lambda E, q=hq, h_=hh, pz_=pz:
                   E.scalar_tensor_tensor(dargb_()[:, 128 * q: 128 * (q + 1)],
                                          pz_[:, 128 * q: 128 * (q + 1)],
                                          sb["smlc"][:, h_: h_ + 1],
                                          sb["tribias"][:, :], ALU.subtract, ALU.add),
                   inc=S["MC"] if hq == 3 else None)
            psb_guard[g % 2] = S["MC"].token()
            if g == 0:
                checkpoint("da%d" % c)
            wt("scalar", S["MC"])
            em("scalar", lambda E: E.activation(dexpb_(), dargb_(), AF.Exp),
               inc=S["SC"])
            if g == 0:
                checkpoint("dx%d" % c)
            for sp in range(2):
                p_ = 2 * g + sp
                yz = py[p_ % 2]
                wt("vector", S["SC"])
                for m in range(2):
                    em("vector", lambda E, q=2 * sp + m, mm=m:
                       E.tensor_tensor(mbuf_(mm), gbuf_(),
                                       dexpb_()[:, 128 * q: 128 * (q + 1)],
                                       ALU.mult),
                       inc=S["MC"] if m == 1 else None)
                if g == 0 and sp == 0:
                    checkpoint("mb%d" % c)
                wt("tensor", S["MC"])
                if py_guard[p_ % 2]:
                    gsem, gval = py_guard[p_ % 2]
                    wt("tensor", gsem, gval)
                for m in range(2):
                    em("tensor", lambda E, pp_=p_, mm=m, yz_=yz:
                       E.matmul(yz_[64 * mm: 64 * (mm + 1), :],
                                XTc_()[:, 128 * pp_ + 64 * mm:
                                       128 * pp_ + 64 * (mm + 1)],
                                mbuf_(mm), start=True, stop=True),
                       inc=S["YP"])
                if g == 0 and sp == 0:
                    checkpoint("ym%d" % c)
                wt("vector", S["YP"])
                em("vector", lambda E, cc=c, pp_=p_, yz_=yz:
                   E.scalar_tensor_tensor(
                       sb["Y2"][:, pp_ * TOK + cc * CH: pp_ * TOK + (cc + 1) * CH],
                       sb["xsc"][:, 128 * pp_: 128 * (pp_ + 1)],
                       sb["Dcolb"][:, pp_:pp_ + 1], yz_[:, :], ALU.mult, ALU.add),
                   inc=S["Y2"])
                py_guard[p_ % 2] = (S["Y2"], S["Y2"].token())
                if g == 0 and sp == 0:
                    checkpoint("as%d" % c)
            if g == 0:
                checkpoint("ga%d" % c)
            if g == 1:
                checkpoint("gb%d" % c)
            if g == 3:
                checkpoint("gc%d" % c)
            if g == 5:
                checkpoint("gd%d" % c)
        em("scalar", lambda E: E.nop(), inc=S["Y2"])

    proj_half(0)
    checkpoint("p1a")
    wt("vector", S["EV"])
    for c in range(4):
        chunk_pipeline(c)
        checkpoint("c%d" % c)
    wt("vector", S["Y2"])
    em("vector", lambda E:
       E.tensor_copy(bass.AP(sb["hb"], 1, [[17 * HBW, 128], [HBW, 17], [1, 3]]),
                     bass.AP(sb["hb"], 513, [[17 * HBW, 128], [HBW, 17], [1, 3]])),
       inc=S["HC"])
    proj_half(1)
    checkpoint("p1b")
    wt("vector", S["EV"])
    for c in range(4, 8):
        chunk_pipeline(c)
        checkpoint("c%d" % c)

    wt("vector", S["CV"])
    em("vector", lambda E: E.memset(sb["lamcum"][:, :], 0.0))
    for c in range(NCHUNK - 2, -1, -1):
        em("vector", lambda E, cc=c: E.tensor_tensor(
            sb["lamcum"][:, cc:cc + 1], sb["lamcum"][:, cc + 1:cc + 2],
            sb["lamlog"][:, cc + 1:cc + 2], ALU.add))
    em("vector", lambda E: E.nop(), inc=S["EX"])
    wt("scalar", S["EX"])
    em("scalar", lambda E: E.activation(sb["E2c"][:, :], sb["lamcum"][:, :], AF.Exp))
    em("scalar", lambda E: E.activation(sb["lamc"][:, :], sb["lamlog"][:, :], AF.Exp),
       inc=S["SC"])
    wt("vector", S["SC"])
    em("vector", lambda E: E.tensor_tensor(sb["lamcum"][:, 0:1], sb["lamcum"][:, 0:1],
                                           sb["lamlog"][:, 0:1], ALU.add), inc=S["EX"])
    wt("tensor", S["EX"])
    if ptrf_guard[0]:
        wt("tensor", S["MC"], ptrf_guard[0])
    em("tensor", lambda E: E.transpose(ptrf[0:1, 0:32], sb["lamcum"][:, 0:1],
                                       sb["identf"][0:32, 0:32]), inc=S["YP"])
    wt("vector", S["YP"])
    em("vector", lambda E: E.tensor_copy(sb["lamT"][:, :], ptrf[0:1, 0:32]), inc=S["EX"])
    wt("sync", S["SC"])
    em("sync", lambda E: E.dma_start(out=e2dram[:, :], in_=sb["E2c"][:, :]),
       inc=S["RB"], k=16)
    em("sync", lambda E: E.dma_start(out=lamdram[:, :], in_=sb["lamc"][:, :]),
       inc=S["RB"], k=16)
    wt("sync", S["RB"])
    for nm, dram in (("E2", e2dram), ("lam", lamdram)):
        dstmap = {"E2": E2rep_, "lam": lamrep_}[nm]
        for hh in range(2):
            for m in range(2):
                em("sync", lambda E, f_=dstmap, d_=dram, h_=hh, mm=m: E.dma_start(
                    out=bass.AP(sb["fa"], (64 * h_ + 32 * mm) * FAP +
                                (2624 if f_ is E2rep_ else 2752),
                                [[FAP, 32], [8, 16], [1, 8]]),
                    in_=bass.AP(d_, 8 * mm, [[0, 32], [16, 16], [1, 8]])),
                    inc=S["RB"], k=16)
    wt("vector", S["RB"])
    wt("vector", S["ST"])
    for p in range(NPAIR):
        rows = slice(0, 64) if p < 8 else slice(64, 128)
        pc = 64 * (p % 8)
        for c in range(NCHUNK):
            cb = (p % 8) * 8 + c
            if c == 0:
                em("vector", lambda E, rr=rows, pc_=pc, cb_=cb, pp_=p:
                   E.tensor_scalar(scc_()[rr, pc_: pc_ + 64],
                                   STB[rr, 64 * cb_: 64 * (cb_ + 1)],
                                   E2rep_()[rr, pp_ * 8: pp_ * 8 + 1],
                                   None, ALU.mult))
            else:
                em("vector", lambda E, rr=rows, pc_=pc, cb_=cb, pp_=p, cc=c:
                   E.scalar_tensor_tensor(scc_()[rr, pc_: pc_ + 64],
                                          STB[rr, 64 * cb_: 64 * (cb_ + 1)],
                                          E2rep_()[rr, pp_ * 8 + cc: pp_ * 8 + cc + 1],
                                          scc_()[rr, pc_: pc_ + 64],
                                          ALU.mult, ALU.add))
    em("vector", lambda E: E.memset(FA[0:3, 1536:2048], 0.0))
    em("vector", lambda E: E.nop(), inc=S["EX"])
    for q in range(8):
        wt("vector", S["EX"])
        if q > 0:
            wt("vector", S["CC"])
        em("vector", lambda E, qq=q: E.tensor_scalar(mskw_()[:, :], scc_()[:, :],
                                                     sb["ind8b"][:, qq:qq + 1],
                                                     None, ALU.mult))
        em("vector", lambda E, qq=q: E.tensor_scalar(sb["lmw"][:, :], sb["lamT"][:, :],
                                                     sb["ind8b"][0:1, qq:qq + 1],
                                                     None, ALU.mult), inc=S["EX"])
        wt("gpsimd", S["EX"])
        em("gpsimd", lambda E, qq=q: E.dma_start(out=ccb[132 * qq: 132 * qq + 128, :],
                                                 in_=mskw_()[:, :]), inc=S["CC"], k=16, dma=True)
        em("gpsimd", lambda E, qq=q: E.dma_start(
            out=ccb[132 * qq + 128: 132 * qq + 129, 0:32],
            in_=sb["lmw"][:, :]), inc=S["CC"], k=16, dma=True)
        em("gpsimd", lambda E, qq=q: E.dma_start(
            out=ccb[132 * qq + 128: 132 * qq + 129, 32:512],
            in_=FA[0:1, 1536:2016]), inc=S["CC"], k=16, dma=True)
        em("gpsimd", lambda E, qq=q: E.dma_start(
            out=ccb[132 * qq + 129: 132 * qq + 132, :],
            in_=FA[0:3, 1536:2048]), inc=S["CC"], k=16, dma=True)
    wt("gpsimd", S["CC"])
    em("gpsimd", lambda E: E.collective_compute(
        "AllReduce", ALU.add, replica_groups=[list(range(8))],
        ins=[bass.AP(ccb, 0, [[1, 1], [1, 8 * 132 * 512]])],
        outs=[bass.AP(ccb, 0, [[1, 1], [1, 8 * 132 * 512]])],
        cc_dim="Partition"), inc=S["CCD"], k=1, dma=True)
    CC_DONE = S["CCD"].token()
    checkpoint("cc")

    barrier()
    checkpoint("exch")

    pgt_guard = [0, 0]
    for j in range(NBLK):
        buf, lsem, ltgt = stream_weight(Win_d, j)
        wt("tensor", lsem, ltgt)
        if pgt_guard[j % 2]:
            wt("tensor", S["SC"], pgt_guard[j % 2])
        for hh in range(2):
            p_ = pp[j % 2] if hh == 0 else psb[j % 2]
            for k in range(KT):
                last = (k == KT - 1)
                em("tensor", lambda E, b=buf, kk=k, pz=p_, h=hh:
                   E.matmul(pz[:, 0:512], b[:, 128 * kk:128 * (kk + 1)],
                            sb["xtb"][:, kk * XPITCH + 3 + NHALF * h:
                                      kk * XPITCH + 3 + NHALF * h + 512],
                            start=(kk == 0), stop=(kk == KT - 1)),
                   inc=S["GA"] if last else None)
        mark_wfree(S["GA"])
        wt("vector", S["GA"])
        for hh in range(2):
            pz = pp[j % 2] if hh == 0 else psb[j % 2]
            em("vector", lambda E, jj=j, h=hh, pz_=pz:
               E.tensor_scalar(FA[:, 512 * (h + 1): 512 * (h + 2)], pz_[:, 0:512],
                               sb["cbir"][:, jj:jj + 1], None, ALU.add),
               inc=S["EV"])
        pgt_guard[j % 2] = S["EV"].token()
        wt("scalar", S["EV"])
        for hh in range(2):
            em("scalar", lambda E, jj=j, h=hh:
               E.activation(sb["sg"][:, jj * TOK + NHALF * h: jj * TOK + NHALF * h + 512],
                            FA[:, 512 * (h + 1): 512 * (h + 2)], AF.Silu), inc=S["SC"])
        checkpoint("gate%d" % j)

    wt("sync", S["CCD"], CC_DONE)
    em("sync", lambda E: E.dma_start(
        out=sb["glamT"][:, :],
        in_=bass.AP(ccb, 128 * 512, [[132 * 512, 8], [1, 32]])), inc=S["GL"], k=16)
    wt("tensor", S["GL"])
    wt("tensor", S["GA"])
    wt("tensor", S["MC"], ptrf_guard[0])
    em("tensor", lambda E: E.matmul(ptrf[0:8, 0:32], sb["selm8b"][:, :],
                                    sb["glamT"][:, :], start=True, stop=True),
       inc=S["YP"])
    wt("scalar", S["YP"])
    em("scalar", lambda E: E.activation(sb["wcomb"][:, :], ptrf[0:8, 0:32], AF.Exp),
       inc=S["SC"])
    wt("vector", S["SC"])
    em("vector", lambda E: E.tensor_scalar(sb["wcomb"][:, :], sb["wcomb"][:, :],
                                           sb["pmask8b"][:, 0:1], None, ALU.mult),
       inc=S["EX"])
    wt("tensor", S["EX"])
    em("tensor", lambda E: E.transpose(ptrf[0:32, 32:40], sb["wcomb"][:, :],
                                       sb["identf"][0:8, 0:8]), inc=S["YP"])
    wt("vector", S["YP"])
    em("vector", lambda E: E.tensor_copy(sb["lamc"][:, :], ptrf[0:32, 32:40]),
       inc=S["EX"])
    wt("sync", S["EX"])
    em("sync", lambda E: E.dma_start(out=wdram[:, :], in_=sb["lamc"][:, :]),
       inc=S["RB"], k=16)
    wt("sync", S["RB"])
    for hh in range(2):
        for m in range(2):
            em("sync", lambda E, h_=hh, mm=m: E.dma_start(
                out=bass.AP(sb["fa"], (64 * h_ + 32 * mm) * FAP + 2880,
                            [[FAP, 32], [8, 16], [1, 8]]),
                in_=bass.AP(wdram, 8 * mm, [[0, 32], [16, 16], [1, 8]])),
                inc=S["RB"], k=16)
    wt("vector", S["RB"])
    gscr_done = [0, 0]
    for r in range(8):
        gi = r % 2
        wt("sync", S["CCD"], CC_DONE)
        if gscr_done[gi]:
            wt("sync", S["GB"], gscr_done[gi])
        em("sync", lambda E, rr=r, g_=gi: E.dma_start(
            out=gscr_(g_), in_=ccb[132 * rr: 132 * rr + 128, :]), inc=S["GL"], k=16)
        wt("vector", S["GL"])
        for p in range(NPAIR):
            rows = slice(0, 64) if p < 8 else slice(64, 128)
            pc = 64 * (p % 8)
            if r == 0:
                em("vector", lambda E, rr_=rows, pc_=pc, pp_=p, g_=gi:
                   E.tensor_scalar(hrun_()[rr_, pc_: pc_ + 64],
                                   gscr_(g_)[rr_, pc_: pc_ + 64],
                                   wrep_()[rr_, pp_ * 8: pp_ * 8 + 1],
                                   None, ALU.mult))
            else:
                em("vector", lambda E, rr_=rows, pc_=pc, pp_=p, g_=gi, rk=r:
                   E.scalar_tensor_tensor(hrun_()[rr_, pc_: pc_ + 64],
                                          gscr_(g_)[rr_, pc_: pc_ + 64],
                                          wrep_()[rr_, pp_ * 8 + rk: pp_ * 8 + rk + 1],
                                          hrun_()[rr_, pc_: pc_ + 64],
                                          ALU.mult, ALU.add))
        em("vector", lambda E: E.nop(), inc=S["GB"])
        gscr_done[gi] = S["GB"].token()
    for c in range(NCHUNK):
        for p in range(NPAIR):
            rows = slice(0, 64) if p < 8 else slice(64, 128)
            pc = 64 * (p % 8)
            cb = (p % 8) * 8 + c
            em("vector", lambda E, rr=rows, pc_=pc, cb_=cb, pp_=p, cc=c:
               E.scalar_tensor_tensor(y1t_()[rr, 0:64], hrun_()[rr, pc_: pc_ + 64],
                                      lamrep_()[rr, pp_ * 8 + cc: pp_ * 8 + cc + 1],
                                      STB[rr, 64 * cb_: 64 * (cb_ + 1)],
                                      ALU.mult, ALU.add))
            em("vector", lambda E, rr=rows, pc_=pc, cb_=cb:
               E.tensor_copy(STB[rr, 64 * cb_: 64 * (cb_ + 1)],
                             hrun_()[rr, pc_: pc_ + 64]))
            em("vector", lambda E, rr=rows, pc_=pc:
               E.tensor_copy(hrun_()[rr, pc_: pc_ + 64], y1t_()[rr, 0:64]))
    em("vector", lambda E: E.nop(), inc=S["EX"])
    H0_DONE = S["EX"].token()

    barrier()
    checkpoint("p3")
    rP.close()

    r45 = ExitStack()
    pyf = [r45.enter_context(nc.psum_tensor("pyf0", [128, 128])),
           r45.enter_context(nc.psum_tensor("pyf1", [128, 128]))]
    pstatC = r45.enter_context(nc.psum_tensor("pstatC", [1, 512]))
    pstat2C = r45.enter_context(nc.psum_tensor("pstat2C", [1, 512]))
    pbc = r45.enter_context(nc.psum_tensor("pbc", [128, 512]))
    po = r45.enter_context(nc.psum_tensor("po", [128, 512]))

    pyf_guard = [0, 0]
    h0s_guard = [0]

    def y_final_chunk(c):
        wt("vector", S["EX"], H0_DONE)
        if c > 0:
            wt("scalar", S["MC"])
        for m in range(2):
            em("scalar", lambda E, cc=c, mm=m: E.dma_start(
                out=bass.AP(sb["ba"], 32 * mm * BAP, [[BAP, 32], [128, NPAIR], [1, CH]]),
                in_=bass.AP(edram, mm * TOK + cc * CH,
                            [[0, 32], [2 * TOK, NPAIR], [1, CH]])), inc=S["DB2"], k=16, dma=True)
            em("scalar", lambda E, cc=c, mm=m: E.dma_start(
                out=bass.AP(sb["ba"], 32 * mm * BAP + 2048,
                            [[BAP, 32], [128, NPAIR], [1, CH]]),
                in_=bass.AP(cmdram, cc * CH, [[TOK, 32], [0, NPAIR], [1, CH]])),
                inc=S["DB2"], k=16, dma=True)
        wt("vector", S["DB2"])
        if c > 0:
            wt("vector", S["Y2"])
        em("vector", lambda E: E.tensor_tensor(chat_(), EB2_(), CMR_(), ALU.mult),
           inc=S["MC"])
        for p in range(NPAIR):
            yz = pyf[p % 2]
            rows = slice(0, 64) if p < 8 else slice(64, 128)
            cb = (p % 8) * 8 + c
            wt("vector", S["MC"])
            if h0s_guard[0]:
                wt("vector", S["YP"], h0s_guard[0])
            em("vector", lambda E, rr=rows, cb_=cb:
               E.tensor_copy(sb["h0s"][:, :], STB[rr, 64 * cb_: 64 * (cb_ + 1)]),
               inc=S["HC"])
            wt("tensor", S["HC"])
            if pyf_guard[p % 2]:
                wt("tensor", S["Y2"], pyf_guard[p % 2])
            for m in range(2):
                em("tensor", lambda E, pp_=p, mm=m, yz_=yz:
                   E.matmul(yz_[64 * mm: 64 * (mm + 1), :],
                            sb["h0s"][32 * mm: 32 * (mm + 1), :],
                            chat_()[32 * mm: 32 * (mm + 1),
                                    128 * pp_: 128 * (pp_ + 1)],
                            start=True, stop=True),
                   inc=S["YP"])
            h0s_guard[0] = S["YP"].token()
            wt("vector", S["YP"])
            em("vector", lambda E, cc=c, pp_=p, yz_=yz:
               E.tensor_tensor(y1b_()[:, :], yz_[:, :],
                               sb["Y2"][:, pp_ * TOK + cc * CH: pp_ * TOK + (cc + 1) * CH],
                               ALU.add))
            em("vector", lambda E, cc=c, pp_=p:
               E.tensor_tensor(sb["Y2"][:, pp_ * TOK + cc * CH: pp_ * TOK + (cc + 1) * CH],
                               y1b_()[:, :],
                               sb["sg"][:, pp_ * TOK + cc * CH: pp_ * TOK + (cc + 1) * CH],
                               ALU.mult), inc=S["Y2"])
            pyf_guard[p % 2] = S["Y2"].token()

    def tail_half(th):
        toff = NHALF * th
        for j in range(NBLK):
            wt("scalar", S["Y2"])
            if j > 0:
                wt("scalar", S["RM"])
            em("scalar", lambda E, jj=j: E.activation(
                rsq16_(), sb["Y2"][:, jj * TOK + toff: jj * TOK + toff + 512],
                AF.Square), inc=S["SC"])
            wt("tensor", S["SC"])
            em("tensor", lambda E, jj=j:
               E.matmul(pstatC[:, :], sb["onescol"][:, :], rsq16_(),
                        start=(jj == 0), stop=(jj == NBLK - 1)),
               inc=S["RM"])
        wt("vector", S["RM"])
        em("vector", lambda E: E.tensor_copy(stat0_(), pstatC[:, :]), inc=S["RM"])
        wt("scalar", S["RM"])
        em("scalar", lambda E: E.activation(stat0_(), stat0_(), AF.Ln,
                                            scale=1.0 / D, bias=sb["epsb"][0:1, 0:1]))
        em("scalar", lambda E: E.activation(stat0_(), stat0_(), AF.Exp, scale=-0.5),
           inc=S["SC"])
        wt("vector", S["SC"])
        em("vector", lambda E: E.memset(onesrow_(), 1.0), inc=S["RM"])
        wt("tensor", S["RM"])
        em("tensor", lambda E: E.matmul(pbc[:, :], onesrow_(), stat0_(),
                                        start=True, stop=True), inc=S["RM"])
        wt("vector", S["RM"])
        em("vector", lambda E: E.tensor_copy(arms_(), pbc[:, :]), inc=S["RM"])
        for j in range(NBLK):
            em("vector", lambda E, jj=j: E.tensor_tensor(
                sb["Y2"][:, jj * TOK + toff: jj * TOK + toff + 512],
                sb["Y2"][:, jj * TOK + toff: jj * TOK + toff + 512], arms_(),
                ALU.mult))
            em("vector", lambda E, jj=j: E.tensor_scalar(
                sb["Y2"][:, jj * TOK + toff: jj * TOK + toff + 512],
                sb["Y2"][:, jj * TOK + toff: jj * TOK + toff + 512],
                sb["nwb"][:, jj:jj + 1], None, ALU.mult), inc=S["RM"])
        YH = S["RM"].token()
        po_guard = [0]
        wt("tensor", S["FF"])
        for j in range(NBLK):
            buf, lsem, ltgt = stream_weight(Wout_d, j)
            wt("sync", S["WO"])
            em("sync", lambda E, jj=j: E.dma_start(
                out=xr_(), in_=xT_d[128 * jj: 128 * (jj + 1), 3 + toff: 3 + toff + 512]),
                inc=S["XR"], k=16)
            wt("tensor", lsem, ltgt)
            wt("tensor", S["RM"], YH)
            if po_guard[0]:
                wt("tensor", S["WO"], po_guard[0])
            for k in range(KT):
                last = (k == KT - 1)
                em("tensor", lambda E, b=buf, kk=k:
                   E.matmul(po[:, :], b[:, 128 * kk: 128 * (kk + 1)],
                            sb["Y2"][:, kk * TOK + toff: kk * TOK + toff + 512],
                            start=(kk == 0), stop=(kk == KT - 1)),
                   inc=S["YP"] if last else None)
            mark_wfree(S["YP"])
            wt("vector", S["YP"])
            wt("vector", S["XR"])
            em("vector", lambda E, jj=j:
               E.tensor_tensor(rH_()[:, jj * 512: (jj + 1) * 512], po[:, :],
                               xr_(), ALU.add), inc=S["WO"])
            po_guard[0] = S["WO"].token()

        def ln(make16, store_out, donesem):
            for j in range(NBLK):
                wt("vector", S[donesem])
                if j > 0:
                    wt("vector", S["L1"])
                em("vector", lambda E, jj=j: E.tensor_copy(
                    rb16_(), rH_()[:, jj * 512:(jj + 1) * 512]), inc=S["L1"])
                wt("scalar", S["L1"])
                em("scalar", lambda E: E.activation(rsq16_(), rb16_(), AF.Square),
                   inc=S["SC"])
                wt("tensor", S["SC"])
                em("tensor", lambda E, jj=j:
                   E.matmul(pstatC[:, :], sb["onescol"][:, :], rb16_(),
                            start=(jj == 0), stop=(jj == NBLK - 1)),
                   inc=S["L1"])
                em("tensor", lambda E, jj=j:
                   E.matmul(pstat2C[:, :], sb["onescol"][:, :], rsq16_(),
                            start=(jj == 0), stop=(jj == NBLK - 1)),
                   inc=S["L1"])
            wt("vector", S["L1"])
            em("vector", lambda E: E.tensor_scalar(stat0_(), pstatC[:, :], 1.0 / D,
                                                   None, ALU.mult))
            em("vector", lambda E: E.tensor_scalar(stat1_(), pstat2C[:, :], 1.0 / D,
                                                   None, ALU.mult))
            em("vector", lambda E: E.tensor_tensor(stat2_(), stat0_(), stat0_(),
                                                   ALU.mult))
            em("vector", lambda E: E.tensor_tensor(stat1_(), stat1_(), stat2_(),
                                                   ALU.subtract), inc=S["L1"])
            wt("scalar", S["L1"])
            em("scalar", lambda E: E.activation(stat1_(), stat1_(), AF.Ln,
                                                bias=sb["epsb"][0:1, 1:2]))
            em("scalar", lambda E: E.activation(stat1_(), stat1_(), AF.Exp,
                                                scale=-0.5), inc=S["SC"])
            wt("vector", S["SC"])
            em("vector", lambda E: E.tensor_tensor(stat2_(), stat0_(), stat1_(),
                                                   ALU.mult), inc=S["L1"])
            wt("tensor", S["L1"])
            em("tensor", lambda E: E.matmul(pbc[:, :], onesrow_(), stat1_(),
                                            start=True, stop=True), inc=S["L1"])
            wt("vector", S["L1"])
            em("vector", lambda E: E.tensor_copy(Ab_(), pbc[:, :]), inc=S["L1"])
            wt("tensor", S["L1"])
            em("tensor", lambda E: E.matmul(pbc[:, :], onesrow_(), stat2_(),
                                            start=True, stop=True), inc=S["L1"])
            wt("vector", S["L1"])
            em("vector", lambda E: E.tensor_copy(Bb_(), pbc[:, :]), inc=S["L1"])
            for j in range(NBLK):
                if store_out:
                    em("vector", lambda E, jj=j: E.tensor_tensor(
                        ost_(), rH_()[:, jj * 512:(jj + 1) * 512], Ab_(), ALU.mult))
                    em("vector", lambda E, jj=j: E.tensor_tensor(
                        ost_(), ost_(), Bb_(), ALU.subtract), inc=S["OD"])
                    wt("sync", S["OD"])
                    em("sync", lambda E, jj=j: E.dma_start(
                        out=out_d[128 * jj: 128 * (jj + 1), toff: toff + 512],
                        in_=ost_()), inc=S["OD"], k=16)
                    wt("vector", S["OD"])
                else:
                    em("vector", lambda E, jj=j: E.tensor_tensor(
                        rH_()[:, jj * 512:(jj + 1) * 512],
                        rH_()[:, jj * 512:(jj + 1) * 512], Ab_(), ALU.mult))
                    em("vector", lambda E, jj=j: E.tensor_tensor(
                        rH_()[:, jj * 512:(jj + 1) * 512],
                        rH_()[:, jj * 512:(jj + 1) * 512], Bb_(), ALU.subtract),
                        inc=S["L1"])
                    if make16:
                        em("vector", lambda E, jj=j: E.tensor_copy(
                            r16_()[:, jj * 512:(jj + 1) * 512],
                            rH_()[:, jj * 512:(jj + 1) * 512]), inc=S["L1"])

        ln(make16=True, store_out=False, donesem="WO")
        po_guard2 = [0]
        for j in range(NBLK):
            buf, lsem, ltgt = stream_weight(WffT_d, j)
            wt("tensor", lsem, ltgt)
            wt("tensor", S["L1"])
            if po_guard2[0]:
                wt("tensor", S["FF"], po_guard2[0])
            for k in range(KT):
                last = (k == KT - 1)
                em("tensor", lambda E, b=buf, kk=k:
                   E.matmul(po[:, :], b[:, 128 * kk: 128 * (kk + 1)],
                            r16_()[:, kk * 512: (kk + 1) * 512],
                            start=(kk == 0), stop=(kk == KT - 1)),
                   inc=S["YP"] if last else None)
            mark_wfree(S["YP"])
            wt("vector", S["YP"])
            em("vector", lambda E, jj=j:
               E.tensor_scalar(ost_(), po[:, :], sb["fsb"][:, jj:jj + 1],
                               sb["fbb"][:, jj:jj + 1], ALU.mult, ALU.add))
            em("vector", lambda E, jj=j:
               E.tensor_tensor(rH_()[:, jj * 512:(jj + 1) * 512],
                               rH_()[:, jj * 512:(jj + 1) * 512], ost_(), ALU.add),
               inc=S["FF"])
            po_guard2[0] = S["FF"].token()
        ln(make16=False, store_out=True, donesem="FF")

    for c in range(NCHUNK):
        y_final_chunk(c)
        checkpoint("y%d" % c)
    tail_half(0)
    checkpoint("t0")
    tail_half(1)

    wt("sync", S["OD"])
    r45.close()
    ctx.close()

    with nc.Block() as block:
        @block.sync
        def _(E):
            for fn in ops["sync"]:
                fn(E)

        @block.gpsimd
        def _(E):
            for fn in ops["gpsimd"]:
                fn(E)

        @block.tensor
        def _(E):
            for fn in ops["tensor"]:
                fn(E)

        @block.vector
        def _(E):
            for fn in ops["vector"]:
                fn(E)

        @block.scalar
        def _(E):
            for fn in ops["scalar"]:
                fn(E)

    return nc


_NC_CACHE = {}


def _host_inputs(inputs):
    f32 = np.float32
    x = np.asarray(inputs["L0_attn_x"], f32)
    Win0 = np.asarray(inputs["L0_attn_W_in"], f32)
    Win = np.zeros((D, 4224), f32)
    Win[:, :4192] = Win0
    b_in = np.asarray(inputs["L0_attn_b_in"], f32)
    conv_w = np.asarray(inputs["L0_attn_conv_w"], f32)
    conv_b = np.asarray(inputs["L0_attn_conv_b"], f32)
    A_log = np.asarray(inputs["L0_attn_A_log"], f32)
    dt_bias = np.asarray(inputs["L0_attn_dt_bias"], f32)
    D_param = np.asarray(inputs["L0_attn_D_param"], f32)
    norm_w = np.asarray(inputs["L0_attn_norm_w"], f32)
    Wout = np.ascontiguousarray(np.asarray(inputs["L0_attn_W_out"], f32))
    ffw = np.asarray(inputs["L0_ffn_weight"], f32)
    ffs = np.asarray(inputs["L0_ffn_scale_weight"], f32)
    ffb = np.asarray(inputs["L0_ffn_bias"], f32)
    WffT = np.ascontiguousarray(ffw.T)

    b_in_r = np.zeros((128, 33), f32)
    for j in range(33):
        nreal = 96 if j == 32 else 128
        b_in_r[:nreal, j] = b_in[128 * j: 128 * j + nreal]
    cwr = np.zeros((128, 68), f32)
    cbr = np.zeros((128, 17), f32)
    for j in range(17):
        nreal = 64 if j == 16 else 128
        for k in range(KCONV):
            cwr[:nreal, 4 * j + k] = conv_w[128 * j: 128 * j + nreal, k]
        cbr[:nreal, j] = conv_b[128 * j: 128 * j + nreal]
    Dcol = np.zeros((128, NPAIR), f32)
    for p in range(NPAIR):
        Dcol[0:64, p] = D_param[2 * p]
        Dcol[64:128, p] = D_param[2 * p + 1]
    nwcol = norm_w.reshape(NBLK, 128).T.astype(f32).copy()
    fscol = ffs.reshape(NBLK, 128).T.astype(f32).copy()
    fbcol = ffb.reshape(NBLK, 128).T.astype(f32).copy()
    tri = np.zeros((128, 128), f32)
    for tp in range(128):
        tri[tp, :tp] = -1e30

    shared = dict(
        Win=Win, Wout=Wout, WffT=WffT, b_in_r=b_in_r, cwr=cwr, cbr=cbr,
        dtb32=dt_bias.reshape(32, 1).astype(f32),
        alog32=A_log.reshape(32, 1).astype(f32),
        Dcol=Dcol, nwcol=nwcol, fscol=fscol, fbcol=fbcol,
        identf_d=np.eye(128, dtype=f32),
        tribias_d=tri,
        epsb_d=np.array([[EPS_M, EPS_LN]], dtype=f32),
    )

    in_maps = []
    for c in range(8):
        b, q = c // 4, c % 4
        t0 = q * TOK
        if q == 0:
            xs_ = np.concatenate([np.zeros((3, D), f32), x[b, 0:TOK]], axis=0)
        else:
            xs_ = x[b, t0 - 3: t0 + TOK]
        xTc = np.ascontiguousarray(xs_.T)
        hflag = np.full((128, 1), 0.0 if q == 0 else 1.0, f32)
        ind8 = np.zeros((128, 8), f32)
        ind8[:, c] = 1.0
        selm8 = np.zeros((8, 8), f32)
        pmask8 = np.zeros((8, 1), f32)
        for g in range(8):
            if g // 4 == b and g % 4 < q:
                pmask8[g, 0] = 1.0
            for gp in range(8):
                if gp // 4 == b and g // 4 == b and (g % 4) < (gp % 4) < q:
                    selm8[gp, g] = 1.0
        m = dict(shared)
        m.update(xT=xTc, hflag=hflag, ind8=ind8, selm8=selm8, pmask8=pmask8)
        in_maps.append(m)
    return in_maps


def kernel(**inputs):
    in_maps = _host_inputs(inputs)
    if "nc" not in _NC_CACHE:
        _NC_CACHE["nc"] = build_nc()
    nc = _NC_CACHE["nc"]
    res = run_bass_kernel_spmd(nc, in_maps, core_ids=list(range(8)))
    out = np.zeros((B, L, D), np.float32)
    for c in range(8):
        b, q = c // 4, c % 4
        out[b, q * TOK:(q + 1) * TOK, :] = np.asarray(res.results[c]["out"]).T
    return out
